# version 50
# baseline (speedup 1.0000x reference)
import math
from contextlib import ExitStack

import numpy as np
import ml_dtypes

import concourse.bass as bass
import concourse.mybir as mybir
from concourse.bass_utils import run_bass_kernel_spmd

F32 = mybir.dt.float32
BF16 = mybir.dt.bfloat16
U32 = mybir.dt.uint32
AF = mybir.ActivationFunctionType
ALU = mybir.AluOpType
AX = mybir.AxisListType
BF = ml_dtypes.bfloat16
f32 = np.float32

B, T, C = 2, 8192, 1024
H, DH, NP = 16, 64, 8
E = 8
NTOK = 2048
SLOTQ = [60, 44, 28, 12]
OFFJ = [0, 60, 104, 132]
NBIAS = 148
NSC = 16
ENGS = ("sync", "gpsimd", "tensor", "scalar", "vector")
EXPBIAS = float(math.log(1.1 / 8.0))


def g_of(j, m):
    return 4 * (3 - j) + (3 - m)


class Builder:
    def __init__(self):
        self.ops = {e: [] for e in ENGS}
        self.cnt = {}
        self.sems = {}
        self.phase_marks = []

    def phase(self, name):
        self.phase_marks.append((name, {e: len(self.ops[e]) for e in ENGS}))

    def add_sem(self, name, handle):
        self.sems[name] = handle
        self.cnt[name] = 0

    def mark(self, s):
        return self.cnt[s]

    def op(self, eng, mk, incs=()):
        caps = []
        for s, k in incs:
            self.cnt[s] += k
            caps.append((self.sems[s], k))

        def fn(Eng, mk=mk, caps=tuple(caps)):
            ins = mk(Eng)
            for h, k in caps:
                ins.then_inc(h, k)
        self.ops[eng].append(fn)

    def wait(self, eng, s, v):
        if v <= 0:
            return
        h = self.sems[s]
        self.ops[eng].append(lambda Eng, h=h, v=v: Eng.wait_ge(h, v))

    _ESEM = {"vector": "dv", "scalar": "ac", "tensor": "pe"}

    def bar(self, eng):
        self.wait(eng, self._ESEM[eng], self.mark(self._ESEM[eng]))


def build_program(truncate=None, dbg=False):
    nc = bass.Bass("TRN2", target_bir_lowering=False, debug=False)

    def din(name, shape, dt=F32):
        return nc.dram_tensor(name, shape, dt, kind="ExternalInput").ap()

    xfull = din("xfull", [T, C])
    xown = din("xown", [NTOK, C])
    wqT = din("wqT", [NP, C, 128])
    wkT = din("wkT", [NP, C, 128])
    wvT = din("wvT", [NP, C, 128])
    wo = din("wo", [C, C])
    wg = din("wg", [E, C, 1024])
    wu = din("wu", [E, C, 1024])
    wd = din("wd", [E, 1024, C])
    wr = din("wr", [128, 64])
    temp = din("temp", [1, 1])
    identf = din("identf", [128, 128])
    identb = din("identb", [128, 128], BF16)
    onesb = din("onesb", [128, 128], BF16)
    onesn = din("onesn", [128, 65], BF16)
    onesf = din("onesf", [1, 128])
    trib = din("trib", [4, 128, 512], BF16)
    eselb = din("eselb", [8, E * 128], BF16)
    biasmask = din("biasmask", [128, NBIAS])
    kdidx = din("kdidx", [128, 4 * NP], U32)
    vdidx = din("vdidx", [128, 16], U32)

    out_own = nc.dram_tensor("out_own", [NTOK, C], F32, kind="ExternalOutput").ap()
    _ik = "ExternalOutput" if dbg else "Internal"
    k_dram = nc.dram_tensor("k_dram", [H * DH, T], BF16, kind=_ik).ap()
    v_dram = nc.dram_tensor("v_dram", [T, NP * 130], BF16, kind=_ik).ap()
    x1_dram = nc.dram_tensor("x1_dram", [NTOK, C], F32, kind=_ik).ap()
    den_dram = nc.dram_tensor("den_dram", [4, NP * 2 * 512], F32).ap()
    rb_dram = nc.dram_tensor("rb_dram", [4, NP * 2 * 512], BF16).ap()

    b = Builder()
    ctx = ExitStack()
    sem_names = ("pe", "ac", "dv", "st", "c0", "ast0", "ast1", "bc0",
                 "kq0", "kq1", "kq2", "kq3", "vq0", "vq1", "vq2", "vq3",
                 "vg0", "vg1", "kg0", "kg1", "wq0",
                 "wo0", "wr0", "wgu0", "wgu1", "wdl0", "wdl1", "x1s", "dfl",
                 "tap", "dn0", "dn1", "dr0", "dl0") + tuple(
                     f"lx{i}" for i in range(8))
    for s in sem_names:
        b.add_sem(s, ctx.enter_context(nc.semaphore(s)))

    def sb(name, shape, dt, side="left"):
        return ctx.enter_context(nc.sbuf_tensor(name, shape, dt, side=side))

    s_identf = sb("s_identf", [128, 128], F32)
    s_identb = sb("s_identb", [128, 128], BF16)
    s_onesb = sb("s_onesb", [128, 128], BF16)
    s_onesn = sb("s_onesn", [128, 65], BF16)
    s_onesf = sb("s_onesf", [1, 128], F32)
    s_temp = sb("s_temp", [1, 1], F32)
    s_xlnfm = sb("s_xlnfm", [128, 8 * NTOK], BF16)
    s_wfm = sb("s_wfm", [8, NTOK], BF16)
    s_mrowb = sb("s_mrowb", [1, 16], BF16)
    s_rt = sb("s_rt", [128, 1], F32)
    s_rt1 = sb("s_rt1", [1, 1], F32)

    rP = ExitStack()
    s_misc = rP.enter_context(nc.sbuf_tensor("s_misc", [128, 64], F32, side="right"))
    s_parts = rP.enter_context(nc.sbuf_tensor("s_parts", [128, 176], F32, side="right"))
    s_biasm = rP.enter_context(nc.sbuf_tensor("s_biasm", [128, NBIAS], F32, side="right"))
    s_mcol = rP.enter_context(nc.sbuf_tensor("s_mcol", [128, 16], F32, side="right"))

    def const_loads():
        pairs = [
            (s_identf[:], identf), (s_identb[:], identb), (s_onesb[:], onesb),
            (s_onesn[:], onesn), (s_onesf[:], onesf),
            (s_biasm[:], biasmask), (s_temp[:], temp),
        ]
        for d, s in pairs:
            b.op("sync", lambda E, d=d, s=s: E.dma_start(out=d, in_=s),
                 [("c0", 16)])
        return b.mark("c0")

    qkv_stack = ExitStack()
    s_wq = qkv_stack.enter_context(nc.sbuf_tensor("s_wq", [128, NP * 1024], BF16))
    s_wk = qkv_stack.enter_context(nc.sbuf_tensor("s_wk", [128, NP * 1024], BF16))
    s_wv = qkv_stack.enter_context(nc.sbuf_tensor("s_wv", [128, NP * 1024], BF16))

    def wslice(wsb, pr, cc):
        return wsb[:, pr * 1024 + cc * 128: pr * 1024 + (cc + 1) * 128]

    a_stack = ExitStack()

    def sbr(name, shape, dt):
        return a_stack.enter_context(nc.sbuf_tensor(name, shape, dt, side="right"))

    xbA = [sbr(f"xbA{i}", [128, 1024], F32) for i in range(8)]
    xlntmA = [sbr(f"xlntmA{i}", [128, 1024], BF16) for i in range(2)]
    statsA = sbr("statsA", [128, 12], F32)
    mvA = sbr("mvA", [128, 2], F32)
    rstdA = sbr("rstdA", [128, 1], F32)
    xlnfmA = [sbr(f"xlnfmA{s}", [128, 8 * 512], BF16) for s in range(2)]
    kstage = [sbr(f"kstage{i}", [128, NP * 512], BF16) for i in range(2)]
    vstage = [sbr(f"vstage{i}", [128, 4 * NP * 130], BF16)
              for i in range(2)]
    vfmt = [sbr(f"vfmt{i}", [128, 512], BF16) for i in range(2)]
    k2t = sbr("k2t", [128, 512], BF16)

    psA = ExitStack()

    def pA(name, shape, dt):
        return psA.enter_context(nc.psum_tensor(name, shape, dt))

    psA_tb = [pA(f"psA_tb{i}", [128, 1024], BF16) for i in range(2)]
    psA_k = [pA(f"psA_k{i}", [128, 512], F32) for i in range(2)]
    psA_v = [pA(f"psA_v{i}", [128, 512], F32) for i in range(2)]
    psA_vt = pA("psA_vt", [128, 512], BF16)
    psA_n = pA("psA_n", [128, 512], F32)

    consts_ld = const_loads()
    for e in ("tensor", "vector", "scalar", "gpsimd"):
        b.wait(e, "c0", consts_ld)

    for wsb, wsrc in ((s_wq, wqT), (s_wk, wkT), (s_wv, wvT)):
        d = wsb[:].rearrange("p (pr cc d) -> p pr cc d", pr=NP, cc=8)
        s = wsrc.rearrange("pr (cc p) d -> p pr cc d", p=128)
        b.op("gpsimd", lambda E, d=d, s=s: E.dma_start(out=d, in_=s),
             [("wq0", 16)])
    qkvw_ld = b.mark("wq0")
    b.wait("tensor", "wq0", qkvw_ld)

    for s2 in range(2):
        ap = vstage[s2][:].rearrange(
            "p (tt pr a n) -> p tt pr a n", tt=4, pr=NP, a=2)[:, :, :, :, 64:65]
        b.op("vector", lambda E, ap=ap: E.memset(ap, 1.0), [("dv", 1)])

    b.phase("pre")
    xmark = {}

    def load_x(tc):
        d = xbA[tc % 8]
        s = xfull[tc * 128:(tc + 1) * 128, :]
        lxs = f"lx{tc % 8}"
        b.op("sync", lambda E, d=d, s=s: E.dma_start(out=d[:], in_=s),
             [(lxs, 16)])
        xmark[tc] = (lxs, b.mark(lxs))

    for tc in range(8):
        load_x(tc)

    lnm = {}
    fm_evac = {}
    kevd = {}
    vevd = {}
    vftm = {}
    nredd = {}
    silm = {}
    stmark = {}
    for sc in range(NSC):
        for k in range(4):
            tc = 4 * sc + k
            xt = xbA[tc % 8]
            b.wait("vector", xmark[tc][0], xmark[tc][1])
            b.op("vector", lambda E, x=xt: E.bn_stats(statsA[:, 0:6], x[:, 0:512]),
                 [("dv", 1)])
            b.op("vector", lambda E, x=xt: E.bn_stats(statsA[:, 6:12],
                 x[:, 512:1024]), [("dv", 1)])
            b.bar("vector")
            b.op("vector", lambda E: E.bn_aggr(mvA[:], statsA[:, 0:12]),
                 [("dv", 1)])
            agg = b.mark("dv")
            b.wait("scalar", "dv", agg)
            b.op("scalar", lambda E: E.activation(rstdA[:], mvA[:, 1:2], AF.Ln,
                 bias=f32(1e-5)), [("ac", 1)])
            b.bar("scalar")
            b.op("scalar", lambda E: E.activation(rstdA[:], rstdA[:], AF.Exp,
                 scale=f32(-0.5)), [("ac", 1)])
            b.wait("vector", "ac", b.mark("ac"))
            b.bar("vector")
            xo = xlntmA[tc % 2]
            b.op("vector", lambda E, o=xo, x=xt: E.tensor_scalar(
                o[:], x[:], mvA[:, 0:1], rstdA[:, 0:1], ALU.subtract, ALU.mult),
                [("dv", 1)])
            lnm[tc] = b.mark("dv")
            if tc + 8 < 4 * NSC:
                b.wait("sync", "dv", lnm[tc])
                load_x(tc + 8)

            def emit_fm(tc2, sc=sc):
                k2 = tc2 % 4
                b.wait("tensor", "dv", lnm[tc2])
                if tc2 - 2 in fm_evac:
                    b.wait("tensor", "dv", fm_evac[tc2 - 2])
                xo2 = xlntmA[tc2 % 2]
                for cc in range(8):
                    o = psA_tb[k2 % 2][:, cc * 128:(cc + 1) * 128]
                    i = xo2[:, cc * 128:(cc + 1) * 128]
                    b.op("tensor", lambda E, o=o, i=i: E.transpose(
                        o, i, s_identb[:]), [("pe", 1)] if cc == 7 else ())
                trm = b.mark("pe")
                b.wait("vector", "pe", trm)
                d = xlnfmA[(tc2 // 4) % 2][:].rearrange(
                    "p (cc n) -> p cc n", cc=8)[:, :, k2 * 128:(k2 + 1) * 128]
                s = psA_tb[k2 % 2][:].rearrange("p (cc n) -> p cc n", cc=8)
                b.op("vector", lambda E, d=d, s=s: E.tensor_copy(d, s),
                     [("dv", 1)])
                fm_evac[tc2] = b.mark("dv")

            if k >= 1:
                emit_fm(tc - 1)
            if k == 3:
                emit_fm(tc)
        fm_done = b.mark("dv")

        if sc >= 2:
            b.wait("tensor", f"ast{sc % 2}", stmark[sc - 2])
        b.wait("tensor", "dv", fm_done)
        for pr in range(NP):
            if (sc, pr - 2) in kevd:
                b.wait("tensor", "dv", kevd[(sc, pr - 2)])
            elif sc > 0 and (sc - 1, pr + NP - 2) in kevd:
                b.wait("tensor", "dv", kevd[(sc - 1, pr + NP - 2)])
            for cc in range(8):
                o = psA_k[pr % 2]
                l = wslice(s_wk, pr, cc)
                r = xlnfmA[sc % 2][:, cc * 512:(cc + 1) * 512]
                b.op("tensor", lambda E, o=o, l=l, r=r, cc=cc: E.matmul(
                    o[:, :], l, r, start=(cc == 0), stop=(cc == 7)),
                    [("pe", 1)] if cc == 7 else ())
            km = b.mark("pe")
            if (sc, pr - 2) in vevd:
                b.wait("tensor", "dv", vevd[(sc, pr - 2)])
            elif sc > 0 and (sc - 1, pr + NP - 2) in vevd:
                b.wait("tensor", "dv", vevd[(sc - 1, pr + NP - 2)])
            for cc in range(8):
                o = psA_v[pr % 2]
                l = wslice(s_wv, pr, cc)
                r = xlnfmA[sc % 2][:, cc * 512:(cc + 1) * 512]
                b.op("tensor", lambda E, o=o, l=l, r=r, cc=cc: E.matmul(
                    o[:, :], l, r, start=(cc == 0), stop=(cc == 7)),
                    [("pe", 1)] if cc == 7 else ())
            vm = b.mark("pe")
            b.wait("vector", "pe", km)
            d = kstage[sc % 2][:, pr * 512:(pr + 1) * 512]
            b.op("vector", lambda E, d=d, s=psA_k[pr % 2]: E.tensor_copy(
                d, s[:, :]), [("dv", 1)])
            kevd[(sc, pr)] = b.mark("dv")
            b.wait("vector", "pe", vm)
            b.op("vector", lambda E, d=vfmt[pr % 2], s=psA_v[pr % 2]:
                 E.tensor_copy(d[:, :], s[:, :]), [("dv", 1)])
            vftm[pr] = b.mark("dv")

            def emit_vtrans(pr2, sc=sc):
                b.wait("tensor", "dv", vftm[pr2])
                if (sc, pr2 - 1) in vevd:
                    b.wait("tensor", "dv", vevd[(sc, pr2 - 1)])
                elif sc > 0:
                    b.wait("tensor", "dv", vevd[(sc - 1, NP - 1)])
                for tt in range(4):
                    o = psA_vt[:, tt * 128:(tt + 1) * 128]
                    i = vfmt[pr2 % 2][:, tt * 128:(tt + 1) * 128]
                    b.op("tensor", lambda E, o=o, i=i: E.transpose(
                        o, i, s_identb[:]), [("pe", 1)] if tt == 3 else ())
                vtm = b.mark("pe")
                b.wait("vector", "pe", vtm)
                dst = vstage[sc % 2][:].rearrange(
                    "p (tt pr2 a n) -> p tt pr2 a n", tt=4, pr2=NP,
                    a=2)[:, :, pr2, :, 0:64]
                srcv = psA_vt[:].rearrange("p (tt a n) -> p tt a n",
                                           tt=4, a=2)
                b.op("vector", lambda E, d=dst, s=srcv: E.tensor_copy(d, s),
                     [("dv", 1)])
                vevd[(sc, pr2)] = b.mark("dv")

            if pr >= 1:
                emit_vtrans(pr - 1)
            if pr == NP - 1:
                emit_vtrans(pr)
            b.wait("scalar", "dv", kevd[(sc, pr)])
            if silm:
                b.wait("scalar", "pe", silm[max(silm)])
            s = kstage[sc % 2][:, pr * 512:(pr + 1) * 512]
            b.op("scalar", lambda E, s=s: E.activation(k2t[:], s, AF.Square),
                 [("ac", 1)])
            k2m = b.mark("ac")
            b.wait("tensor", "ac", k2m)
            if nredd:
                b.wait("tensor", "dv", nredd[max(nredd)])
            b.op("tensor", lambda E: E.matmul(psA_n[0:65, :], s_onesn[:], k2t[:],
                 start=True, stop=True), [("pe", 1)])
            silm[(sc, pr)] = b.mark("pe")
            b.wait("vector", "pe", silm[(sc, pr)])
            d = s_parts[0:65, pr * 16 + sc:pr * 16 + sc + 1]
            b.op("vector", lambda E, d=d: E.reduce_max(d, psA_n[0:65, :],
                 axis=AX.X), [("dv", 1)])
            nredd[(sc, pr)] = b.mark("dv")
        kv_evacs = b.mark("dv")
        b.wait("sync", "dv", kv_evacs)
        asts = f"ast{sc % 2}"
        dko = k_dram.rearrange("(pr p) t -> p pr t", p=128)[
            :, :, sc * 512:(sc + 1) * 512]
        sko = kstage[sc % 2][:].rearrange("p (pr n) -> p pr n", pr=NP)
        b.op("sync", lambda E, d=dko, s=sko: E.dma_start(out=d, in_=s),
             [(asts, 16)])
        for tt in range(4):
            d = v_dram[sc * 512 + tt * 128:sc * 512 + (tt + 1) * 128, :]
            s = vstage[sc % 2][:, tt * NP * 130:(tt + 1) * NP * 130]
            b.op("sync", lambda E, d=d, s=s: E.dma_start(out=d, in_=s),
                 [(asts, 16)])
        stmark[sc] = b.mark(asts)
    kv_done_ast = (b.mark("ast0"), b.mark("ast1"))

    b.bar("vector")
    b.op("vector", lambda E: E.tensor_reduce(
        s_parts[0:65, 128:136],
        s_parts[0:65, 0:128].rearrange("p (pr s) -> p pr s", pr=NP),
        AX.X, ALU.max), [("dv", 1)])
    b.bar("vector")
    b.op("vector", lambda E: E.tensor_copy(s_misc[0:1, 0:8],
         s_parts[0:1, 128:136]), [("dv", 1)])
    b.op("vector", lambda E: E.tensor_copy(s_misc[0:1, 8:16],
         s_parts[64:65, 128:136]), [("dv", 1)])

    psA.close()
    a_stack.close()

    b.phase("A")
    r2 = ExitStack()

    def sbr2(name, shape, dt):
        return r2.enter_context(nc.sbuf_tensor(name, shape, dt, side="right"))

    x_tm = [sbr2(f"x_tm{i}", [128, 1024], F32) for i in range(4)]
    r3 = ExitStack()
    s_biasall = r3.enter_context(nc.sbuf_tensor(
        "s_biasall", [128, H * NBIAS], BF16, side="right"))
    q_fm = [r3.enter_context(nc.sbuf_tensor(f"q_fm{i}", [128, NTOK], BF16,
                                            side="right")) for i in range(NP)]

    ps2 = ExitStack()
    pstA2 = [ps2.enter_context(nc.psum_tensor(f"pstA2_{i}", [128, 1024], BF16))
             for i in range(2)]
    psqA2 = [ps2.enter_context(nc.psum_tensor(f"psqA2_{i}", [128, 512], F32))
             for i in range(2)]
    psnA2 = ps2.enter_context(nc.psum_tensor("psnA2", [128, 512], F32))

    a2_stack = ExitStack()

    def sbr4(name, shape, dt):
        return a2_stack.enter_context(
            nc.sbuf_tensor(name, shape, dt, side="right"))

    xlntm2 = [sbr4(f"xlntm2_{i}", [128, 1024], BF16) for i in range(2)]
    stats2 = sbr4("stats2", [128, 12], F32)
    mv2 = sbr4("mv2", [128, 2], F32)
    rstd2 = sbr4("rstd2", [128, 1], F32)
    q2t = sbr4("q2t", [128, 512], BF16)

    xm2 = {}

    def load_x2(ot):
        d = x_tm[ot % 4]
        s = xown[ot * 128:(ot + 1) * 128, :]
        lxs = f"lx{ot % 4}"
        b.op("sync", lambda E, d=d, s=s: E.dma_start(out=d[:], in_=s),
             [(lxs, 16)])
        xm2[ot] = (lxs, b.mark(lxs))

    b.wait("sync", "ast0", kv_done_ast[0])
    b.wait("sync", "ast1", kv_done_ast[1])
    for ot in range(4):
        load_x2(ot)
    fm2_evac = {}
    lnA2 = {}
    for ot in range(16):
        b.wait("vector", xm2[ot][0], xm2[ot][1])
        xt = x_tm[ot % 4]
        b.op("vector", lambda E, x=xt: E.bn_stats(stats2[:, 0:6], x[:, 0:512]),
             [("dv", 1)])
        b.op("vector", lambda E, x=xt: E.bn_stats(stats2[:, 6:12], x[:, 512:1024]),
             [("dv", 1)])
        b.bar("vector")
        b.op("vector", lambda E: E.bn_aggr(mv2[:], stats2[:, 0:12]), [("dv", 1)])
        b.wait("scalar", "dv", b.mark("dv"))
        b.op("scalar", lambda E: E.activation(rstd2[:], mv2[:, 1:2], AF.Ln,
             bias=f32(1e-5)), [("ac", 1)])
        b.bar("scalar")
        b.op("scalar", lambda E: E.activation(rstd2[:], rstd2[:], AF.Exp,
             scale=f32(-0.5)), [("ac", 1)])
        b.wait("vector", "ac", b.mark("ac"))
        b.bar("vector")
        xo = xlntm2[ot % 2]
        b.op("vector", lambda E, o=xo, x=xt: E.tensor_scalar(
            o[:], x[:], mv2[:, 0:1], rstd2[:, 0:1], ALU.subtract, ALU.mult),
            [("dv", 1)])
        ln2 = b.mark("dv")
        lnA2[ot] = ln2
        if ot + 4 < 16:
            b.wait("sync", "dv", ln2)
            load_x2(ot + 4)

        def emit_fm2(ot2):
            b.wait("tensor", "dv", lnA2[ot2])
            if ot2 - 2 in fm2_evac:
                b.wait("tensor", "dv", fm2_evac[ot2 - 2])
            xo2 = xlntm2[ot2 % 2]
            for cc in range(8):
                o = pstA2[ot2 % 2][:, cc * 128:(cc + 1) * 128]
                i = xo2[:, cc * 128:(cc + 1) * 128]
                b.op("tensor", lambda E, o=o, i=i: E.transpose(
                    o, i, s_identb[:]), [("pe", 1)] if cc == 7 else ())
            trm = b.mark("pe")
            b.wait("vector", "pe", trm)
            d = s_xlnfm[:].rearrange("p (cc n) -> p cc n", cc=8)[
                :, :, ot2 * 128:(ot2 + 1) * 128]
            s = pstA2[ot2 % 2][:].rearrange("p (cc n) -> p cc n", cc=8)
            b.op("vector", lambda E, d=d, s=s: E.tensor_copy(d, s),
                 [("dv", 1)])
            fm2_evac[ot2] = b.mark("dv")

        if ot >= 1:
            emit_fm2(ot - 1)
        if ot == 15:
            emit_fm2(15)
    fm2_done = b.mark("dv")
    b.wait("tensor", "dv", fm2_done)

    qevm = {}
    for pr in range(NP):
        for tt in range(4):
            key = pr * 4 + tt
            if key - 2 in qevm:
                b.wait("tensor", "dv", qevm[key - 2])
            for cc in range(8):
                o = psqA2[tt % 2]
                l = wslice(s_wq, pr, cc)
                r = s_xlnfm[:, cc * NTOK + tt * 512:
                            cc * NTOK + (tt + 1) * 512]
                b.op("tensor", lambda E, o=o, l=l, r=r, cc=cc: E.matmul(
                    o[:, :], l, r, start=(cc == 0), stop=(cc == 7)),
                    [("pe", 1)] if cc == 7 else ())
            qm = b.mark("pe")
            b.wait("vector", "pe", qm)
            d = q_fm[pr][:, tt * 512:(tt + 1) * 512]
            b.op("vector", lambda E, d=d, s=psqA2[tt % 2]: E.tensor_copy(
                d, s[:, :]), [("dv", 1)])
            qevm[key] = b.mark("dv")
    bias_done = b.mark("dv")

    tap1 = 0
    if dbg and dbg >= 2:
        dbg_misc = nc.dram_tensor("dbg_misc", [128, 64], F32,
                                  kind="ExternalOutput").ap()
        dbg_bias = nc.dram_tensor("dbg_bias", [128, H * NBIAS], BF16,
                                  kind="ExternalOutput").ap()
        dbg_qfm = nc.dram_tensor("dbg_qfm", [NP, 128, NTOK], BF16,
                                 kind="ExternalOutput").ap()
        dbg_xlnA2 = nc.dram_tensor("dbg_xlnA2", [8, 128, NTOK], BF16,
                                   kind="ExternalOutput").ap()
        b.wait("sync", "dv", bias_done)
        b.op("sync", lambda E: E.dma_start(out=dbg_misc, in_=s_misc[:]),
             [("tap", 16)])
        b.op("sync", lambda E: E.dma_start(out=dbg_bias, in_=s_biasall[:]),
             [("tap", 16)])
        for i in range(NP):
            b.op("sync", lambda E, i=i: E.dma_start(
                out=dbg_qfm[i], in_=q_fm[i][:]), [("tap", 16)])
        for i in range(8):
            b.op("sync", lambda E, i=i: E.dma_start(
                out=dbg_xlnA2[i],
                in_=s_xlnfm[:, i * NTOK:(i + 1) * NTOK]), [("tap", 16)])
        tap1 = b.mark("tap")

    a2_stack.close()
    ps2.close()
    qkv_stack.close()

    b.phase("A2")
    b_outer = ExitStack()
    b_stack = ExitStack()

    def sbl(name, shape, dt):
        return b_stack.enter_context(nc.sbuf_tensor(name, shape, dt))

    y_fm = [b_outer.enter_context(nc.sbuf_tensor(
        f"y_fm{i}", [128, NTOK], BF16)) for i in range(NP)]
    s_trib = sbl("s_trib", [128, 4 * 512], BF16)
    s_kdidx = sbl("s_kdidx", [128, 4 * NP], U32)
    s_vdidx = sbl("s_vdidx", [128, 16], U32)
    ksbB = [sbl(f"ksbB{i}", [128, 512], BF16) for i in range(4)]
    vsbB = [sbl(f"vsbB{i}", [128, 520], BF16) for i in range(4)]
    kdgB = [sbl(f"kdgB{i}", [128, 512], BF16) for i in range(2)]
    vdgB = [[sbl(f"vdgB{jj}_{i}", [128, NP * 130], BF16) for i in range(4)]
            for jj in range(2)]
    expB = [sbl(f"expB{i}", [128, 512], BF16) for i in range(6)]
    s_woB = sbl("s_woB", [128, NP * 1024], BF16)
    ofmC = [sbl(f"ofmC{i}", [128, 512], BF16) for i in range(2)]
    xlntmC = [sbl(f"xlntmC{i}", [128, 1024], BF16) for i in range(2)]
    statsC = sbl("statsC", [128, 12], F32)
    mvC = sbl("mvC", [128, 2], F32)
    rstdC = sbl("rstdC", [128, 1], F32)
    s_wrB = sbl("s_wrB", [128, 64], BF16)
    rn = {}
    for nm in ("zz", "ex1", "inh", "dist", "zhi", "zone", "w_", "fb2", "fbx",
               "topm", "fbw"):
        rn[nm] = sbl(f"r_{nm}", [128, 8], F32)
    for nm in ("wsum", "ne", "mx1", "mx2", "fbs", "rec"):
        rn[nm] = sbl(f"r_{nm}", [128, 1], F32)
    wtmB = sbl("wtmB", [128, 8], BF16)
    wtmAll = sbl("wtmAll", [128, 16 * 8], BF16)
    s_wrF = sbl("s_wrF", [128, 64], F32)
    den_st = [sbl(f"den_st{i}", [1, 512], F32) for i in range(2)]
    denT = sbl("denT", [128, 64], F32)
    denR = sbl("denR", [128, 64], BF16)
    rbAll = sbl("rbAll", [1, NP * 2 * 512], BF16)
    ps3 = ExitStack()
    pssB = [ps3.enter_context(nc.psum_tensor(f"pssB{i}", [128, 512], F32))
            for i in range(4)]
    psavB = [ps3.enter_context(nc.psum_tensor(f"psavB{i}", [128, 512], F32))
             for i in range(2)]
    psoC = ps3.enter_context(nc.psum_tensor("psoC", [128, 512], F32))
    pscB = ps3.enter_context(nc.psum_tensor("pscB", [128, 1024], BF16))

    for dd, ss in ((s_trib[:].rearrange("p (t n) -> p t n", t=4),
                    trib.rearrange("t p n -> p t n")),
                   (s_kdidx[:], kdidx), (s_vdidx[:], vdidx)):
        b.op("sync", lambda E, d=dd, s=ss: E.dma_start(out=d, in_=s),
             [("bc0", 16)])
    bconst_ld = b.mark("bc0")
    b.wait("gpsimd", "bc0", bconst_ld)
    b.wait("vector", "bc0", bconst_ld)
    for qn in ("gpsimd", "sync"):
        b.wait(qn, "ast0", kv_done_ast[0])
        b.wait(qn, "ast1", kv_done_ast[1])
    b.wait("scalar", "dv", bias_done)
    b.wait("tensor", "dv", bias_done)

    GC = float(1.0 / math.e)
    GL = float(0.5 - math.log(4.0 / 3.0))
    GU = 0.5
    if dbg and dbg >= 3:
        dbg_rt = nc.dram_tensor("dbg_rt", [16, 128, 96], F32,
                                kind="ExternalOutput").ap()
        dbg_wrB = nc.dram_tensor("dbg_wrB", [128, 64], BF16,
                                 kind="ExternalOutput").ap()
        dbg_srt = nc.dram_tensor("dbg_srt", [128, 1], F32,
                                 kind="ExternalOutput").ap()
    b.op("vector", lambda E: E.reciprocal(s_rt1[:], s_temp[:]), [("dv", 1)])
    b.op("sync", lambda E: E.dma_start(out=s_wrF[:], in_=wr), [("wr0", 16)])
    b.wait("vector", "wr0", b.mark("wr0"))
    b.bar("vector")
    b.op("vector", lambda E: E.tensor_copy(s_wrB[:], s_wrF[:]), [("dv", 1)])
    wrm = b.mark("dv")
    b.wait("tensor", "dv", wrm)
    b.op("tensor", lambda E: E.matmul(psoC[:, 0:1], s_onesf[:, :], s_rt1[:],
         start=True, stop=True), [("pe", 1)])
    b.wait("vector", "pe", b.mark("pe"))
    b.op("vector", lambda E: E.tensor_copy(s_rt[:], psoC[:, 0:1]), [("dv", 1)])
    rtm = b.mark("dv")
    if dbg and dbg >= 3:
        b.wait("sync", "dv", max(rtm, wrm))
        b.op("sync", lambda E: E.dma_start(out=dbg_wrB, in_=s_wrB[:]),
             [("tap", 16)])
        b.op("sync", lambda E: E.dma_start(out=dbg_srt, in_=s_rt[:]),
             [("tap", 16)])

    psoC_last = [rtm]
    pscB_last = [0]
    ofm_free = [0, 0]
    slot_tile_free = {}
    slot_lxm = {}
    wfm_ev = {}
    xln2ev = {}
    pending_C = []

    def piece_norm_pre(j):
        def emit():
            dns = f"dn{j % 2}"
            b.wait("sync", dns, b.mark(dns))
            if denT_free[0]:
                b.wait("sync", "dv", denT_free[0])
            s = den_dram[j].rearrange("(p n) -> p n", p=128)
            b.op("sync", lambda E, s=s: E.dma_start(out=denT[:], in_=s),
                 [("dr0", 16)])
            ldm = b.mark("dr0")
            b.wait("vector", "dr0", ldm)
            b.bar("vector")
            b.op("vector", lambda E: E.reciprocal(denT[:], denT[:]),
                 [("dv", 1)])
            b.bar("vector")
            b.op("vector", lambda E: E.tensor_copy(denR[:], denT[:]),
                 [("dv", 1)])
            rm = b.mark("dv")
            denT_free[0] = rm
            b.wait("sync", "dv", rm)
            dd = rb_dram[j].rearrange("(p n) -> p n", p=128)
            b.op("sync", lambda E, d=dd: E.dma_start(out=d, in_=denR[:]),
                 [("dr0", 16)])
            b.wait("sync", "dr0", b.mark("dr0"))
            if (j - 1, NP - 1) in norm_bm:
                b.wait("sync", "pe", norm_bm[(j - 1, NP - 1)])
            b.op("sync", lambda E: E.dma_start(out=rbAll[:, :],
                 in_=rb_dram[j]), [("dl0", 16)])
            rbAll_ld[0] = b.mark("dl0")
        return emit

    def piece_norm(j, pr):
        def emit():
            b.wait("tensor", "dl0", rbAll_ld[0])
            b.wait("tensor", "dv", psoC_last[0])
            for half in (0, 1):
                r = rbAll[0:1, (pr * 2 + half) * 512:
                          (pr * 2 + half + 1) * 512]
                b.op("tensor", lambda E, r=r, half=half: E.matmul(
                    psoC[half * 64:half * 64 + 64, :],
                    s_onesb[0:1, 0:64], r,
                    start=True, stop=True), [("pe", 1)])
            bm = b.mark("pe")
            norm_bm[(j, pr)] = bm
            b.wait("vector", "pe", bm)
            for half in (0, 1):
                d = y_fm[pr][half * 64:half * 64 + 64,
                             j * 512:(j + 1) * 512]
                b.op("vector", lambda E, d=d, half=half: E.tensor_tensor(
                    d, d, psoC[half * 64:half * 64 + 64, :], ALU.mult),
                    [("dv", 1)])
            psoC_last[0] = b.mark("dv")
        return emit

    def piece_loads(j):
        def emit():
            for tb in range(4):
                ot = 4 * j + tb
                if j == 0:
                    b.wait("sync", "dv", lnA2[12 + tb])
                else:
                    fdv, fst = slot_tile_free[tb]
                    b.wait("sync", "dv", fdv)
                    b.wait("sync", "x1s", fst)
                d = x_tm[tb]
                s = xown[ot * 128:(ot + 1) * 128, :]
                lxs = f"lx{tb}"
                b.op("sync", lambda E, d=d, s=s: E.dma_start(
                    out=d[:], in_=s), [(lxs, 16)])
                slot_lxm[tb] = (lxs, b.mark(lxs))
        return emit

    def piece_outproj(j, cc, ydone):
        def emit():
            b.wait("tensor", "wo0", 16)
            b.wait("tensor", "dv", psoC_last[0])
            for pr in range(NP):
                l = s_woB[:, pr * 1024 + cc * 128:pr * 1024 + (cc + 1) * 128]
                r = y_fm[pr][:, j * 512:(j + 1) * 512]
                b.op("tensor", lambda E, l=l, r=r, pr=pr: E.matmul(
                    psoC[:, :], l, r, start=(pr == 0), stop=(pr == NP - 1)),
                    [("pe", 1)] if pr == NP - 1 else ())
            om = b.mark("pe")
            b.wait("vector", "pe", om)
            if ofm_free[cc % 2]:
                b.wait("vector", "pe", ofm_free[cc % 2])
            b.op("vector", lambda E, d=ofmC[cc % 2]: E.tensor_copy(
                d[:, :], psoC[:, :]), [("dv", 1)])
            ev = b.mark("dv")
            psoC_last[0] = ev
            b.wait("tensor", "dv", ev)
            b.wait("tensor", "dv", pscB_last[0])
            for tb in range(4):
                o = pscB[:, tb * 128:(tb + 1) * 128]
                i = ofmC[cc % 2][:, tb * 128:(tb + 1) * 128]
                b.op("tensor", lambda E, o=o, i=i: E.transpose(
                    o, i, s_identb[:]), [("pe", 1)] if tb == 3 else ())
            trm = b.mark("pe")
            ofm_free[cc % 2] = trm
            b.wait("vector", "pe", trm)
            if cc == 0:
                for tb in range(4):
                    b.wait("vector", slot_lxm[tb][0], slot_lxm[tb][1])
            for tb in range(4):
                xap = x_tm[tb][:, cc * 128:(cc + 1) * 128]
                pap = pscB[:, tb * 128:(tb + 1) * 128]
                b.op("vector", lambda E, x=xap, p=pap: E.tensor_tensor(
                    x, x, p, ALU.add), [("dv", 1)] if tb == 3 else ())
            pscB_last[0] = b.mark("dv")
        return emit

    def piece_tile(j, tb):
        def emit():
            ot = 4 * j + tb
            xt = x_tm[tb]
            b.bar("vector")
            b.op("vector", lambda E, x=xt: E.bn_stats(
                statsC[:, 0:6], x[:, 0:512]), [("dv", 1)])
            b.op("vector", lambda E, x=xt: E.bn_stats(
                statsC[:, 6:12], x[:, 512:1024]), [("dv", 1)])
            b.bar("vector")
            b.op("vector", lambda E: E.bn_aggr(mvC[:], statsC[:, 0:12]),
                 [("dv", 1)])
            b.wait("scalar", "dv", b.mark("dv"))
            b.op("scalar", lambda E: E.activation(rstdC[:], mvC[:, 1:2],
                 AF.Ln, bias=f32(1e-5)), [("ac", 1)])
            b.bar("scalar")
            b.op("scalar", lambda E: E.activation(rstdC[:], rstdC[:], AF.Exp,
                 scale=f32(-0.5)), [("ac", 1)])
            b.wait("vector", "ac", b.mark("ac"))
            b.bar("vector")
            xo = xlntmC[ot % 2]
            b.op("vector", lambda E, o=xo, x=xt: E.tensor_scalar(
                o[:], x[:], mvC[:, 0:1], rstdC[:, 0:1],
                ALU.subtract, ALU.mult), [("dv", 1)])
            ln2 = b.mark("dv")
            b.wait("sync", "dv", ln2)
            dx1 = x1_dram[ot * 128:(ot + 1) * 128, :]
            b.op("sync", lambda E, d=dx1, s=xt: E.dma_start(
                out=d, in_=s[:]), [("x1s", 16)])
            stm = b.mark("x1s")
            slot_tile_free[tb] = (ln2, stm)
            b.wait("tensor", "dv", ln2)
            b.wait("tensor", "dv", pscB_last[0])
            for cc in range(8):
                o = pscB[:, cc * 128:(cc + 1) * 128]
                i = xo[:, cc * 128:(cc + 1) * 128]
                b.op("tensor", lambda E, o=o, i=i: E.transpose(
                    o, i, s_identb[:]), [("pe", 1)] if cc == 7 else ())
            trm = b.mark("pe")
            b.wait("vector", "pe", trm)
            dfm = s_xlnfm[:].rearrange("p (cc n) -> p cc n", cc=8)[
                :, :, ot * 128:(ot + 1) * 128]
            sfm = pscB[:].rearrange("p (cc n) -> p cc n", cc=8)
            b.op("vector", lambda E, d=dfm, s=sfm: E.tensor_copy(d, s),
                 [("dv", 1)])
            xle = b.mark("dv")
            xln2ev[ot] = xle
            pscB_last[0] = xle
            b.wait("tensor", "dv", psoC_last[0])
            b.wait("tensor", "dv", b.mark("dv"))
            b.wait("tensor", "ac", b.mark("ac"))
            b.wait("vector", "pe", b.mark("pe"))
            b.wait("scalar", "dv", b.mark("dv"))
            b.wait("scalar", "pe", b.mark("pe"))
            for cc in range(8):
                o = psoC[:, 0:8]
                l = s_xlnfm[:, cc * NTOK + ot * 128:cc * NTOK + (ot + 1) * 128]
                r = s_wrB[:, cc * 8:(cc + 1) * 8]
                b.op("tensor", lambda E, o=o, l=l, r=r, cc=cc: E.matmul(
                    o, l, r, start=(cc == 0), stop=(cc == 7)),
                    [("pe", 1)] if cc == 7 else ())
            zm = b.mark("pe")
            b.wait("vector", "pe", zm)
            V = "vector"
            b.bar(V)
            b.op(V, lambda E: E.tensor_scalar(rn["zz"][:], psoC[:, 0:8],
                 s_rt[:, 0:1], None, ALU.mult), [("dv", 1)])
            zzm = b.mark("dv")
            psoC_last[0] = zzm
            b.wait("scalar", "dv", zzm)
            b.op("scalar", lambda E: E.activation(rn["ex1"][:], rn["zz"][:],
                 AF.Exp, scale=f32(-1.0)), [("ac", 1)])
            b.wait(V, "ac", b.mark("ac"))
            b.bar(V)
            b.op(V, lambda E: E.tensor_scalar(rn["inh"][:], rn["ex1"][:], 1.0,
                 None, ALU.add), [("dv", 1)])
            b.bar(V)
            b.op(V, lambda E: E.reciprocal(rn["inh"][:], rn["inh"][:]),
                 [("dv", 1)])
            b.bar(V)
            b.op(V, lambda E: E.tensor_scalar(rn["dist"][:], rn["inh"][:], GC,
                 None, ALU.subtract), [("dv", 1)])
            b.bar(V)
            b.op(V, lambda E: E.scalar_tensor_tensor(rn["dist"][:],
                 rn["dist"][:], -1.0, rn["dist"][:], ALU.mult, ALU.max),
                 [("dv", 1)])
            b.bar(V)
            b.op(V, lambda E: E.tensor_scalar(rn["zhi"][:], rn["inh"][:], GU,
                 None, ALU.is_le), [("dv", 1)])
            b.bar(V)
            b.op(V, lambda E: E.scalar_tensor_tensor(rn["zone"][:],
                 rn["inh"][:], GL, rn["zhi"][:], ALU.is_ge, ALU.mult),
                 [("dv", 1)])
            dm = b.mark("dv")
            b.wait("scalar", "dv", dm)
            b.op("scalar", lambda E: E.activation(rn["w_"][:], rn["dist"][:],
                 AF.Exp, scale=f32(-10.0)), [("ac", 1)])
            b.op("scalar", lambda E: E.activation(rn["fb2"][:], rn["dist"][:],
                 AF.Exp, scale=f32(-1.0 / 0.3)), [("ac", 1)])
            b.wait(V, "ac", b.mark("ac"))
            b.bar(V)
            b.op(V, lambda E: E.tensor_tensor(rn["w_"][:], rn["w_"][:],
                 rn["zone"][:], ALU.mult), [("dv", 1)])
            b.bar(V)
            b.op(V, lambda E: E.tensor_reduce(rn["wsum"][:], rn["w_"][:], AX.X,
                 ALU.add), [("dv", 1)])
            b.bar(V)
            b.op(V, lambda E: E.tensor_scalar(rn["ne"][:], rn["wsum"][:], 1e-8,
                 None, ALU.is_lt), [("dv", 1)])
            b.bar(V)
            b.op(V, lambda E: E.reduce_max(rn["mx1"][:], rn["fb2"][:],
                 axis=AX.X), [("dv", 1)])
            b.bar(V)
            b.op(V, lambda E: E.tensor_scalar(rn["fbx"][:], rn["fb2"][:],
                 rn["mx1"][:, 0:1], None, ALU.is_equal), [("dv", 1)])
            b.bar(V)
            b.op(V, lambda E: E.scalar_tensor_tensor(rn["fbx"][:], rn["fbx"][:],
                 -1e30, rn["fb2"][:], ALU.mult, ALU.add), [("dv", 1)])
            b.bar(V)
            b.op(V, lambda E: E.reduce_max(rn["mx2"][:], rn["fbx"][:],
                 axis=AX.X), [("dv", 1)])
            b.bar(V)
            b.op(V, lambda E: E.tensor_scalar(rn["topm"][:], rn["fb2"][:],
                 rn["mx2"][:, 0:1], None, ALU.is_ge), [("dv", 1)])
            b.bar(V)
            b.op(V, lambda E: E.tensor_tensor(rn["fbw"][:], rn["fb2"][:],
                 rn["topm"][:], ALU.mult), [("dv", 1)])
            b.bar(V)
            b.op(V, lambda E: E.tensor_reduce(rn["fbs"][:], rn["fbw"][:], AX.X,
                 ALU.add), [("dv", 1)])
            b.bar(V)
            b.op(V, lambda E: E.tensor_scalar(rn["fbs"][:], rn["fbs"][:], 1e-8,
                 None, ALU.max), [("dv", 1)])
            b.bar(V)
            b.op(V, lambda E: E.reciprocal(rn["rec"][:], rn["fbs"][:]),
                 [("dv", 1)])
            b.bar(V)
            b.op(V, lambda E: E.tensor_scalar(rn["fbw"][:], rn["fbw"][:],
                 rn["rec"][:, 0:1], None, ALU.mult), [("dv", 1)])
            b.bar(V)
            b.op(V, lambda E: E.tensor_tensor(rn["fbx"][:], rn["fbw"][:],
                 rn["w_"][:], ALU.subtract), [("dv", 1)])
            b.bar(V)
            b.op(V, lambda E: E.scalar_tensor_tensor(rn["w_"][:], rn["fbx"][:],
                 rn["ne"][:, 0:1], rn["w_"][:], ALU.mult, ALU.add),
                 [("dv", 1)])
            b.bar(V)
            b.op(V, lambda E: E.tensor_reduce(rn["wsum"][:], rn["w_"][:], AX.X,
                 ALU.add), [("dv", 1)])
            b.bar(V)
            b.op(V, lambda E: E.tensor_scalar(rn["wsum"][:], rn["wsum"][:],
                 1e-8, None, ALU.max), [("dv", 1)])
            b.bar(V)
            b.op(V, lambda E: E.reciprocal(rn["rec"][:], rn["wsum"][:]),
                 [("dv", 1)])
            b.bar(V)
            b.op(V, lambda E: E.tensor_scalar(wtmB[:], rn["w_"][:],
                 rn["rec"][:, 0:1], None, ALU.mult), [("dv", 1)])
            wm = b.mark("dv")
            if dbg and dbg >= 3:
                b.wait("sync", "dv", wm)
                flds = ["zz", "ex1", "inh", "dist", "zhi", "zone", "w_", "fb2",
                        "fbx", "topm", "fbw"]
                for fi, nm in enumerate(flds):
                    dd = dbg_rt[ot, :, fi * 8:(fi + 1) * 8]
                    b.op("sync", lambda E, d=dd, s=rn[nm]: E.dma_start(
                        out=d, in_=s[:]), [("tap", 16)])
                b.wait("vector", "tap", b.mark("tap"))
            b.bar(V)
            b.op(V, lambda E, ot=ot: E.tensor_copy(
                wtmAll[:, ot * 8:(ot + 1) * 8], wtmB[:]), [("dv", 1)])
            wfm_ev[ot] = b.mark("dv")
        return emit

    vdg_free = {0: 0, 1: 0}
    av_free = [0] * 2
    den_free = {(p, h): 0 for p in range(2) for h in range(2)}
    denT_free = [0]
    rbAll_ld = [0]
    norm_bm = {}
    pending_epi = None
    kdg_free = [0, 0]
    prev_pair_av = 0
    prev2_pair_av = 0
    prev3_pair_av = 0
    prev_pair_emark = 0
    pair_idx = 0
    kdram_flat = k_dram.rearrange("r (q c) -> (r q) c", c=512)
    for j in range(4):
        nq = SLOTQ[j]
        ncnk = nq + 4
        nquad = nq // 4
        if vdg_free[j % 2]:
            b.wait("gpsimd", "pe", vdg_free[j % 2])
        vgs = f"vg{j % 2}"
        for t in range(4):
            idx = s_vdidx[:, j * 4 + t:j * 4 + t + 1]
            b.op("gpsimd", lambda E, t=t, idx=idx, jj=j % 2:
                 E.indirect_dma_start(
                out=vdgB[jj][t][:], out_offset=None, in_=v_dram,
                in_offset=bass.IndirectOffsetOnAxis(ap=idx, axis=0)),
                [(vgs, 16)])
        vdgm = (vgs, b.mark(vgs))
        for pr in range(NP):
            if kdg_free[pr % 2]:
                b.wait("gpsimd", "pe", kdg_free[pr % 2])
            idx = s_kdidx[:, j * NP + pr:j * NP + pr + 1]
            kgs = f"kg{pr % 2}"
            b.op("gpsimd", lambda E, pr=pr, idx=idx: E.indirect_dma_start(
                out=kdgB[pr % 2][:], out_offset=None, in_=kdram_flat,
                in_offset=bass.IndirectOffsetOnAxis(ap=idx, axis=0)),
                [(kgs, 16)])
            kdgm = (kgs, b.mark(kgs))
            qmark = {}

            pp = pair_idx % 2

            def load_quad(q, pr=pr, pp=pp):
                bi = pp * 2 + q % 2
                dK = ksbB[bi][:]
                sK = k_dram[pr * 128:(pr + 1) * 128, q * 512:(q + 1) * 512]
                kqs = f"kq{bi}"
                vqs = f"vq{bi}"
                b.op("sync", lambda E, d=dK, s=sK: E.dma_start(out=d, in_=s),
                     [(kqs, 16)])
                dV = vsbB[bi][:].rearrange("p (c w) -> p c w", c=4)
                sV = v_dram[q * 512:(q + 1) * 512,
                            pr * 130:(pr + 1) * 130].rearrange(
                                "(c p) w -> p c w", p=128)
                b.op("gpsimd", lambda E, d=dV, s=sV: E.dma_start(out=d, in_=s),
                     [(vqs, 16)])
                qmark[q] = (kqs, b.mark(kqs), vqs, b.mark(vqs))

            if prev2_pair_av:
                b.wait("sync", "pe", prev2_pair_av)
                b.wait("gpsimd", "pe", prev2_pair_av)
            for q in range(min(2, nquad)):
                load_quad(q)
            h1, h2 = 2 * pr, 2 * pr + 1
            qc = slice(j * 512, (j + 1) * 512)
            smark, emark, avm, trimark = {}, {}, {}, {}

            def emit_qk(c, pr=pr, j=j, pp=pp):
                diag = c >= nq
                t = c - nq
                par = c % 2
                if c >= 2:
                    b.wait("tensor", "ac", emark[c - 2])
                else:
                    b.wait("tensor", "ac", prev_pair_emark)
                if diag:
                    b.wait("tensor", kdgm[0], kdgm[1])
                    kl0 = kdgB[pr % 2][0:64, t * 128:(t + 1) * 128]
                    kl1 = kdgB[pr % 2][64:128, t * 128:(t + 1) * 128]
                else:
                    q4 = c // 4
                    if c % 4 == 0:
                        b.wait("tensor", qmark[q4][0], qmark[q4][1])
                        b.wait("tensor", qmark[q4][2], qmark[q4][3])
                    kb = ksbB[pp * 2 + q4 % 2]
                    kl0 = kb[0:64, (c % 4) * 128:(c % 4 + 1) * 128]
                    kl1 = kb[64:128, (c % 4) * 128:(c % 4 + 1) * 128]
                r0 = q_fm[pr][0:64, qc]
                r1 = q_fm[pr][64:128, qc]
                b.op("tensor", lambda E, o=pssB[0 + par], l=kl0, r=r0: E.matmul(
                    o[:, :], l, r, start=True, stop=True))
                b.op("tensor", lambda E, o=pssB[2 + par], l=kl1, r=r1: E.matmul(
                    o[:, :], l, r, start=True, stop=True), [("pe", 1)])
                smark[c] = b.mark("pe")
                b.wait("scalar", "pe", smark[c])
                if c - 3 in avm:
                    b.wait("scalar", "pe", avm[c - 3])
                elif prev_pair_av:
                    b.wait("scalar", "pe", prev_pair_av)
                for half in (0, 1):
                    hh = h1 if half == 0 else h2
                    colb = (144 + t) if diag else (OFFJ[j] + c)
                    eb = expB[2 * (c % 3) + half]
                    bias_ap = s_biasm[:, colb:colb + 1]
                    src = pssB[(0 + par) if half == 0 else (2 + par)]
                    b.op("scalar", lambda E, o=eb, s=src, ba=bias_ap:
                         E.activation(o[:], s[:, :], AF.Exp, bias=ba,
                                      scale=f32(0.125)), [("ac", 1)])
                emark[c] = b.mark("ac")
                if diag:
                    b.wait("vector", "ac", emark[c])
                    for half in (0, 1):
                        eb = expB[2 * (c % 3) + half]
                        tm = s_trib[:, t * 512:(t + 1) * 512]
                        b.op("vector", lambda E, eb=eb, tm=tm: E.tensor_tensor(
                            eb[:], eb[:], tm, ALU.mult), [("dv", 1)])
                    trimark[c] = b.mark("dv")

            def emit_av(c, pr=pr, j=j, pp=pp):
                diag = c >= nq
                t = c - nq
                par = c % 2
                if diag:
                    b.wait("tensor", "dv", trimark[c])
                    if t == 0:
                        b.wait("tensor", vdgm[0], vdgm[1])
                else:
                    b.wait("tensor", "ac", emark[c])
                if c == 0:
                    for fm in av_free:
                        b.wait("tensor", "dv", fm)
                first, last = (c == 0), (c == ncnk - 1)
                for half in (0, 1):
                    eb = expB[2 * (c % 3) + half]
                    if diag:
                        vl = vdgB[j % 2][t][:, pr * 130 + half * 65:
                                            pr * 130 + half * 65 + 65]
                    else:
                        vl = vsbB[pp * 2 + (c // 4) % 2][
                            :, (c % 4) * 130 + half * 65:
                            (c % 4) * 130 + half * 65 + 65]
                    o = psavB[half]
                    b.op("tensor", lambda E, o=o, vl=vl, es=eb,
                         first=first, last=last: E.matmul(
                             o[0:65, :], vl, es[:, :], start=first, stop=last),
                         [("pe", 1)] if half == 1 else ())
                avm[c] = b.mark("pe")
                if (not diag) and (c % 4 == 3):
                    q4 = c // 4
                    if q4 + 2 <= nquad - 1:
                        b.wait("sync", "pe", avm[c])
                        b.wait("gpsimd", "pe", avm[c])
                        load_quad(q4 + 2)

            emit_qk(0)
            emit_qk(1)
            emit_qk(2)
            if pending_epi is not None:
                pending_epi()
                pending_epi = None
            for _ in range(3):
                if pending_C:
                    pending_C.pop(0)()
            emit_av(0)
            for c in range(3, ncnk):
                emit_qk(c)
                emit_av(c - 2)
            emit_av(ncnk - 2)
            emit_av(ncnk - 1)
            av_done = avm[ncnk - 1]
            kdg_free[pr % 2] = av_done
            def make_epilogue(pr=pr, qc=qc, av_done=av_done, j=j):
                def emit_epilogue():
                    dns = f"dn{j % 2}"
                    b.wait("vector", "pe", av_done)
                    for half in (0, 1):
                        b.bar("vector")
                        d = y_fm[pr][half * 64:half * 64 + 64, qc]
                        b.op("vector", lambda E, d=d, s=psavB[half]:
                             E.tensor_copy(d, s[0:64, :]), [("dv", 1)])
                        if den_free[(j % 2, half)]:
                            b.wait("vector", dns, den_free[(j % 2, half)])
                        b.bar("vector")
                        b.op("vector", lambda E, o=den_st[half],
                             s=psavB[half]: E.tensor_copy(
                            o[:, :], s[64:65, :]), [("dv", 1)])
                        cm = b.mark("dv")
                        av_free[half] = cm
                        b.wait("sync", "dv", cm)
                        dd = den_dram[j, (pr * 2 + half) * 512:
                                      (pr * 2 + half + 1) * 512]
                        b.op("sync", lambda E, d=dd, s=den_st[half]:
                             E.dma_start(out=d, in_=s[:]), [(dns, 16)])
                        den_free[(j % 2, half)] = b.mark(dns)
                return emit_epilogue
            pending_epi = make_epilogue()
            prev2_pair_av = prev_pair_av
            prev_pair_av = av_done
            prev_pair_emark = b.mark("ac")
            pair_idx += 1
        if pending_epi is not None:
            pending_epi()
            pending_epi = None
        slot_ydone = b.mark("dv")
        if j == 0:
            dwo = s_woB[:].rearrange("p (pr c) -> p pr c", pr=NP)
            swo = wo.rearrange("(pr p) c -> p pr c", p=128)
            b.op("gpsimd", lambda E, d=dwo, s=swo: E.dma_start(out=d, in_=s),
                 [("wo0", 16)])
        pending_C.append(piece_norm_pre(j))
        for pr2 in range(NP):
            pending_C.append(piece_norm(j, pr2))
        pending_C.append(piece_loads(j))
        for cc in range(8):
            pending_C.append(piece_outproj(j, cc, None))
        for tb in range(4):
            pending_C.append(piece_tile(j, tb))
        vdg_free[j % 2] = b.mark("pe")
    if pending_epi is not None:
        pending_epi()
        pending_epi = None
    while pending_C:
        pending_C.pop(0)()
    b.wait("tensor", "dv", b.mark("dv"))
    wtr = {}
    for ot in range(16):
        if ot >= 2:
            b.wait("tensor", "dv", wtr[ot - 2])
        o = pscB[0:8, (ot % 2) * 128:(ot % 2) * 128 + 128]
        b.op("tensor", lambda E, o=o, ot=ot: E.transpose(
            o, wtmAll[:, ot * 8:(ot + 1) * 8], s_identb[:]), [("pe", 1)])
        b.wait("vector", "pe", b.mark("pe"))
        dfm2 = s_wfm[0:8, ot * 128:(ot + 1) * 128]
        b.op("vector", lambda E, d=dfm2, o=o: E.tensor_copy(d, o),
             [("dv", 1)])
        wtr[ot] = b.mark("dv")
    y_done = b.mark("dv")
    wfm_done = y_done
    x1_stored = b.mark("x1s")
    tap2 = 0
    if dbg and dbg >= 2:
        dbg_yfm = nc.dram_tensor("dbg_yfm", [NP, 128, NTOK], BF16,
                                 kind="ExternalOutput").ap()
        dbg_wfm = nc.dram_tensor("dbg_wfm", [8, NTOK], BF16,
                                 kind="ExternalOutput").ap()
        dbg_xlnC = nc.dram_tensor("dbg_xlnC", [8, 128, NTOK], BF16,
                                  kind="ExternalOutput").ap()
        b.wait("sync", "dv", wfm_done)
        for i in range(NP):
            b.op("sync", lambda E, i=i: E.dma_start(
                out=dbg_yfm[i], in_=y_fm[i][:]), [("tap", 16)])
        b.op("sync", lambda E: E.dma_start(out=dbg_wfm, in_=s_wfm[0:8, :]),
             [("tap", 16)])
        for i in range(8):
            b.op("sync", lambda E, i=i: E.dma_start(
                out=dbg_xlnC[i],
                in_=s_xlnfm[:, i * NTOK:(i + 1) * NTOK]), [("tap", 16)])
        tap2 = b.mark("tap")
    ps3.close()
    b_stack.close()
    b_outer.close()
    r3.close()
    r2.close()
    rP.close()

    b.phase("C2")
    d_stack = ExitStack()

    def sbd(name, shape, dt):
        return d_stack.enter_context(nc.sbuf_tensor(name, shape, dt))

    s_eselb = sbd("s_eselb", [8, E * 128], BF16)
    wgbD = [sbd(f"wgbD{i}", [128, 8192], BF16) for i in range(2)]
    wubD = [sbd(f"wubD{i}", [128, 8192], BF16) for i in range(2)]
    wdbD = sbd("wdbD", [128, 8192], BF16)
    hsbD = [sbd(f"hsbD{i}", [128, 512], BF16) for i in range(8)]
    pswS = [sbd(f"pswS{i}", [128, 512], BF16) for i in range(2)]
    accsD = [[sbd(f"accsD{t}_{i}", [128, 512], F32) for i in range(8)]
             for t in range(4)]
    x1tD = sbd("x1tD", [128, 1024], F32)
    outtD = sbd("outtD", [128, 1024], F32)
    ps5 = ExitStack()
    pswD = ps5.enter_context(nc.psum_tensor("pswD", [128, 512], F32))
    psgD = [ps5.enter_context(nc.psum_tensor(f"psgD{i}", [128, 512], F32))
            for i in range(2)]
    psuD = [ps5.enter_context(nc.psum_tensor(f"psuD{i}", [128, 512], F32))
            for i in range(2)]
    psdD = [ps5.enter_context(nc.psum_tensor(f"psdD{i}", [128, 512], F32))
            for i in range(2)]
    pstrD = ps5.enter_context(nc.psum_tensor("pstrD", [128, 512], F32))

    def ldw(e, which, buf):
        src = {"g": wg, "u": wu, "d": wd}[which]
        sem = (f"wdl{e % 2}" if which == "d" else f"wgu{e % 2}")
        dap = buf[:].rearrange("p (cc i) -> p cc i", cc=8)
        sap = src[e].rearrange("(cc p) i -> p cc i", p=128)
        b.op("gpsimd", lambda E, d=dap, s=sap: E.dma_start(out=d, in_=s),
             [(sem, 16)])
        return (sem, b.mark(sem))

    if dbg and dbg >= 2:
        b.wait("gpsimd", "tap", tap2)
    b.wait("sync", "dv", wfm_done)
    b.wait("gpsimd", "dv", wfm_done)
    b.op("sync", lambda E: E.dma_start(out=s_eselb[:], in_=eselb),
         [("bc0", 16)])
    b.wait("tensor", "bc0", b.mark("bc0"))
    b.wait("vector", "dv", wfm_done)
    b.wait("tensor", "dv", wfm_done)
    gm, um, dmk = {}, {}, {}
    gm[0] = ldw(0, "g", wgbD[0])
    um[0] = ldw(0, "u", wubD[0])
    dmk[0] = ldw(0, "d", wdbD)
    acc_ready = {tt: 0 for tt in range(4)}
    silum = {}
    hmul = {}
    accev = {}
    downm = {}
    psg_last = {}
    psu_last = {}
    psd_last = {}
    pswD_last = [0]
    psw_last = {0: 0, 1: 0}
    prev_down = 0
    for e in range(E):
        if e == 1:
            b.phase("De1")
        if e == 3:
            b.phase("De3")
        if e + 1 < E:
            if e >= 1:
                b.wait("gpsimd", "pe", downm[(e - 1, 3)])
            gm[e + 1] = ldw(e + 1, "g", wgbD[(e + 1) % 2])
            um[e + 1] = ldw(e + 1, "u", wubD[(e + 1) % 2])
        b.wait("tensor", um[e][0], um[e][1])
        b.wait("tensor", gm[e][0], gm[e][1])
        for tt in range(4):
            if pswD_last[0]:
                b.wait("tensor", "dv", pswD_last[0])
            wf = s_wfm[0:8, tt * 512:(tt + 1) * 512]
            es = s_eselb[:, e * 128:(e + 1) * 128]
            b.op("tensor", lambda E, l=es, r=wf: E.matmul(
                pswD[:, :], l, r, start=True, stop=True), [("pe", 1)])
            bcm = b.mark("pe")
            b.wait("vector", "pe", bcm)
            if psw_last[tt % 2]:
                b.wait("vector", "dv", psw_last[tt % 2])
            b.op("vector", lambda E, o=pswS[tt % 2]: E.tensor_copy(
                o[:, :], pswD[:, :]), [("dv", 1)])
            pswD_last[0] = b.mark("dv")
            pswev = pswD_last[0]
            if e == 0 and tt == 0:
                b.phase("Dbc")
            if prev_down:
                b.wait("scalar", "pe", prev_down)
                b.wait("vector", "pe", prev_down)
            for ic in range(8):
                if psg_last.get(ic % 2):
                    b.wait("tensor", "ac", psg_last[ic % 2])
                for cc in range(8):
                    l = wgbD[e % 2][:, cc * 1024 + ic * 128:
                                    cc * 1024 + (ic + 1) * 128]
                    r = s_xlnfm[:, cc * NTOK + tt * 512:cc * NTOK + (tt + 1) * 512]
                    b.op("tensor", lambda E, l=l, r=r, cc=cc,
                         psg=psgD[ic % 2]: E.matmul(
                        psg[:, :], l, r, start=(cc == 0), stop=(cc == 7)),
                        [("pe", 1)] if cc == 7 else ())
                gmk = b.mark("pe")
                if psu_last.get(ic % 2):
                    b.wait("tensor", "dv", psu_last[ic % 2])
                for cc in range(8):
                    l = wubD[e % 2][:, cc * 1024 + ic * 128:
                                    cc * 1024 + (ic + 1) * 128]
                    r = s_xlnfm[:, cc * NTOK + tt * 512:cc * NTOK + (tt + 1) * 512]
                    b.op("tensor", lambda E, l=l, r=r, cc=cc,
                         psu=psuD[ic % 2]: E.matmul(
                        psu[:, :], l, r, start=(cc == 0), stop=(cc == 7)),
                        [("pe", 1)] if cc == 7 else ())
                umk = b.mark("pe")
                b.wait("scalar", "pe", gmk)
                hap = hsbD[ic]
                b.op("scalar", lambda E, h=hap, psg=psgD[ic % 2]: E.activation(
                    h[:, 0:512], psg[:, :], AF.Silu), [("ac", 1)])
                silum[(e, tt, ic)] = b.mark("ac")
                psg_last[ic % 2] = silum[(e, tt, ic)]
                b.wait("vector", "ac", silum[(e, tt, ic)])
                b.wait("vector", "pe", umk)
                b.op("vector", lambda E, h=hap, psu=psuD[ic % 2]:
                     E.tensor_tensor(
                    h[:, 0:512], h[:, 0:512], psu[:, :], ALU.mult),
                    [("dv", 1)])
                b.bar("vector")
                b.op("vector", lambda E, h=hap, o=pswS[tt % 2]: E.tensor_tensor(
                    h[:, 0:512], h[:, 0:512], o[:, :], ALU.mult), [("dv", 1)])
                hmul[(e, tt, ic)] = b.mark("dv")
                psw_last[tt % 2] = hmul[(e, tt, ic)]
                psu_last[ic % 2] = hmul[(e, tt, ic)]
                if e == 0 and tt == 0 and ic == 0:
                    b.phase("Dh0")
                if e == 0 and tt == 0 and ic == 7:
                    b.phase("Dh7")
            b.wait("tensor", dmk[e][0], dmk[e][1])
            b.wait("tensor", "dv", hmul[(e, tt, 7)])
            for dc in range(8):
                if (tt, dc) in accev and e > 0:
                    b.wait("tensor", "dv", accev[(tt, dc)])
                if psd_last.get(dc % 2):
                    b.wait("tensor", "dv", psd_last[dc % 2])
                for ic in range(8):
                    l = wdbD[:, ic * 1024 + dc * 128:ic * 1024 + (dc + 1) * 128]
                    r = hsbD[ic][:, 0:512]
                    b.op("tensor", lambda E, l=l, r=r, ic=ic,
                         psd=psdD[dc % 2]: E.matmul(
                        psd[:, :], l, r, start=(ic == 0),
                        stop=(ic == 7)), [("pe", 1)] if ic == 7 else ())
                dmm = b.mark("pe")
                if e == 0 and tt == 0 and dc == 0:
                    b.phase("Ddm")
                b.wait("vector", "pe", dmm)
                acc = accsD[tt][dc]
                if e == 0:
                    b.op("vector", lambda E, a=acc, psd=psdD[dc % 2]:
                         E.tensor_copy(a[:, :], psd[:, :]), [("dv", 1)])
                else:
                    b.op("vector", lambda E, a=acc, psd=psdD[dc % 2]:
                         E.tensor_tensor(
                        a[:, :], a[:, :], psd[:, :], ALU.add),
                        [("dv", 1)])
                accev[(tt, dc)] = b.mark("dv")
                psd_last[dc % 2] = accev[(tt, dc)]
                if e == 0 and tt == 0 and dc == 0:
                    b.phase("Dd0")
            downm[(e, tt)] = b.mark("pe")
            prev_down = downm[(e, tt)]
            if e == 0 and tt == 0:
                b.phase("Dtt0")
            acc_ready[tt] = b.mark("dv")
        if e + 1 < E:
            b.wait("gpsimd", "pe", downm[(e, 3)])
            dmk[e + 1] = ldw(e + 1, "d", wdbD)

    b.phase("Dfin")
    stq = {}
    trum = {}
    b.wait("sync", "x1s", x1_stored)
    for tt in range(4):
        b.wait("tensor", "dv", acc_ready[tt])
        for tb in range(4):
            ot = tt * 4 + tb
            dl = x1_dram[ot * 128:(ot + 1) * 128, :]
            b.op("sync", lambda E, d=x1tD, s=dl: E.dma_start(out=d[:], in_=s),
                 [("dfl", 16)])
            xm = b.mark("dfl")
            for cc in range(8):
                if trum:
                    b.wait("tensor", "dv", trum[max(trum)])
                o = pstrD[:, 0:128]
                i = accsD[tt][cc][:, tb * 128:(tb + 1) * 128]
                b.op("tensor", lambda E, o=o, i=i: E.transpose(o, i, s_identf[:]),
                     [("pe", 1)])
                trm = b.mark("pe")
                b.wait("vector", "pe", trm)
                if cc == 0:
                    b.wait("vector", "dfl", xm)
                    if ot >= 1:
                        b.wait("vector", "st", stq.get(ot - 1, 0))
                dap = outtD[:, cc * 128:(cc + 1) * 128]
                xap = x1tD[:, cc * 128:(cc + 1) * 128]
                b.op("vector", lambda E, d=dap, x=xap: E.tensor_tensor(
                    d, x, pstrD[:, 0:128], ALU.add),
                    [("dv", 1)])
                trum[(tt, tb, cc)] = b.mark("dv")
            om = b.mark("dv")
            b.wait("sync", "dv", om)
            dout = out_own[ot * 128:(ot + 1) * 128, :]
            b.op("sync", lambda E, d=dout: E.dma_start(out=d, in_=outtD[:]),
                 [("st", 16)])
            stq[ot] = b.mark("st")
    b.wait("sync", "st", b.mark("st"))
    ps5.close()
    d_stack.close()

    b.phase("D")
    opl = b.ops
    if truncate is not None:
        cut = dict(b.phase_marks)[truncate]
        opl = {e: b.ops[e][:cut[e]] for e in ENGS}
    block = ctx.enter_context(nc.Block())
    for eng in ENGS:
        def body(Eng, _ops=opl[eng]):
            for fn in _ops:
                fn(Eng)
        getattr(block, eng)(body)
    ctx.close()
    return nc


def _consts():
    identf = np.eye(128, dtype=np.float32)
    identb = np.eye(128, dtype=BF)
    onesb = np.ones((128, 128), dtype=BF)
    onesn = np.zeros((128, 65), dtype=BF)
    onesn[0:64, 0] = 1.0
    onesn[64:128, 64] = 1.0
    onesf = np.ones((1, 128), dtype=np.float32)
    trib = np.zeros((4, 128, 512), dtype=np.float32)
    p = np.arange(128)[:, None]
    fidx = np.arange(512)[None, :]
    for t in range(4):
        trib[t] = (128 * t + p <= fidx).astype(np.float32)
    trib = trib.astype(BF)
    eselb = np.zeros((8, E * 128), dtype=BF)
    for e in range(E):
        eselb[e, e * 128:(e + 1) * 128] = 1.0
    return identf, identb, onesb, onesn, onesf, trib, eselb


def _percore(m):
    biasmask = np.full((128, NBIAS), -30.0, dtype=np.float32)
    for j in range(4):
        qb = 512 * g_of(j, m)
        for i in range(SLOTQ[j]):
            if 128 * (i + 1) > qb:
                biasmask[:, OFFJ[j] + i] = -1e30
    kdidx = np.zeros((128, 4 * NP), dtype=np.uint32)
    vdidx = np.zeros((128, 16), dtype=np.uint32)
    for j in range(4):
        g = g_of(j, m)
        for pr in range(NP):
            rows = pr * 128 + np.arange(128)
            kdidx[:, j * NP + pr] = rows * 16 + g
        for t in range(4):
            vdidx[:, j * 4 + t] = 512 * g + 128 * t + np.arange(128)
    return biasmask, kdidx, vdidx


_NC_CACHE = {}


def kernel(L0_attn_x_q, L0_attn_w_q, L0_attn_w_k, L0_attn_w_v, L0_attn_w_o,
           L0_ffn_Wg, L0_ffn_Wu, L0_ffn_Wd, L0_ffn_Wr, L0_ffn_temperature):
    x = np.ascontiguousarray(np.asarray(L0_attn_x_q, dtype=np.float32))
    w_q = np.asarray(L0_attn_w_q, dtype=np.float32)
    w_k = np.asarray(L0_attn_w_k, dtype=np.float32)
    w_v = np.asarray(L0_attn_w_v, dtype=np.float32)
    w_o = np.ascontiguousarray(np.asarray(L0_attn_w_o, dtype=np.float32))
    Wg = np.ascontiguousarray(np.asarray(L0_ffn_Wg, dtype=np.float32))
    Wu = np.ascontiguousarray(np.asarray(L0_ffn_Wu, dtype=np.float32))
    Wd = np.ascontiguousarray(np.asarray(L0_ffn_Wd, dtype=np.float32))
    Wr = np.ascontiguousarray(np.asarray(L0_ffn_Wr, dtype=np.float32))
    tmp = np.asarray(L0_ffn_temperature, dtype=np.float32).reshape(1, 1)

    def pairT(w):
        return np.ascontiguousarray(
            w.reshape(NP, 2, DH, C).transpose(0, 3, 1, 2).reshape(NP, C, 128))

    wqT, wkT, wvT = pairT(w_q), pairT(w_k), pairT(w_v)
    wrp = np.ascontiguousarray(
        Wr.reshape(8, 128, E).transpose(1, 0, 2).reshape(128, 8 * E))
    identf, identb, onesb, onesn, onesf, trib, eselb = _consts()

    if "nc" not in _NC_CACHE:
        _NC_CACHE["nc"] = build_program()
    nc = _NC_CACHE["nc"]

    in_maps = []
    for r in range(8):
        s, m = divmod(r, 4)
        xs = x[s]
        xo = np.concatenate(
            [xs[512 * g_of(j, m):512 * (g_of(j, m) + 1)] for j in range(4)])
        biasmask, kdidx, vdidx = _percore(m)
        in_maps.append(dict(
            xfull=xs, xown=np.ascontiguousarray(xo), wqT=wqT, wkT=wkT, wvT=wvT,
            wo=w_o, wg=Wg, wu=Wu, wd=Wd, wr=wrp, temp=tmp,
            identf=identf, identb=identb, onesb=onesb, onesn=onesn, onesf=onesf,
            trib=trib, eselb=eselb, biasmask=biasmask, kdidx=kdidx, vdidx=vdidx))

    res = run_bass_kernel_spmd(nc, in_maps, core_ids=list(range(8)))
    out = np.empty((B, T, C), dtype=np.float32)
    for r in range(8):
        s, m = divmod(r, 4)
        oo = np.asarray(res.results[r]["out_own"])
        for j in range(4):
            g = g_of(j, m)
            out[s, 512 * g:512 * (g + 1)] = oo[512 * j:512 * (j + 1)]
    return out


# version 51
# speedup vs baseline: 1.0804x; 1.0804x over previous
import math
from contextlib import ExitStack

import numpy as np
import ml_dtypes

import concourse.bass as bass
import concourse.mybir as mybir
from concourse.bass_utils import run_bass_kernel_spmd

F32 = mybir.dt.float32
BF16 = mybir.dt.bfloat16
U32 = mybir.dt.uint32
AF = mybir.ActivationFunctionType
ALU = mybir.AluOpType
AX = mybir.AxisListType
BF = ml_dtypes.bfloat16
f32 = np.float32

B, T, C = 2, 8192, 1024
H, DH, NP = 16, 64, 8
E = 8
NTOK = 2048
SLOTQ = [60, 44, 28, 12]
OFFJ = [0, 60, 104, 132]
NBIAS = 148
NSC = 16
ENGS = ("sync", "gpsimd", "tensor", "scalar", "vector")
EXPBIAS = float(math.log(1.1 / 8.0))


def g_of(j, m):
    return 4 * (3 - j) + (3 - m)


class Builder:
    def __init__(self):
        self.ops = {e: [] for e in ENGS}
        self.cnt = {}
        self.sems = {}
        self.phase_marks = []

    def phase(self, name):
        self.phase_marks.append((name, {e: len(self.ops[e]) for e in ENGS}))

    def add_sem(self, name, handle):
        self.sems[name] = handle
        self.cnt[name] = 0

    def mark(self, s):
        return self.cnt[s]

    def op(self, eng, mk, incs=()):
        caps = []
        for s, k in incs:
            self.cnt[s] += k
            caps.append((self.sems[s], k))

        def fn(Eng, mk=mk, caps=tuple(caps)):
            ins = mk(Eng)
            for h, k in caps:
                ins.then_inc(h, k)
        self.ops[eng].append(fn)

    def wait(self, eng, s, v):
        if v <= 0:
            return
        h = self.sems[s]
        self.ops[eng].append(lambda Eng, h=h, v=v: Eng.wait_ge(h, v))

    _ESEM = {"vector": "dv", "scalar": "ac", "tensor": "pe"}

    def bar(self, eng):
        self.wait(eng, self._ESEM[eng], self.mark(self._ESEM[eng]))


def build_program(truncate=None, dbg=False):
    nc = bass.Bass("TRN2", target_bir_lowering=False, debug=False)

    def din(name, shape, dt=F32):
        return nc.dram_tensor(name, shape, dt, kind="ExternalInput").ap()

    xfull = din("xfull", [T, C])
    xown = din("xown", [NTOK, C])
    wqT = din("wqT", [NP, C, 128])
    wkT = din("wkT", [NP, C, 128])
    wvT = din("wvT", [NP, C, 128])
    wo = din("wo", [C, C])
    wg = din("wg", [E, C, 1024])
    wu = din("wu", [E, C, 1024])
    wd = din("wd", [E, 1024, C])
    wr = din("wr", [128, 64])
    temp = din("temp", [1, 1])
    identf = din("identf", [128, 128])
    identb = din("identb", [128, 128], BF16)
    onesb = din("onesb", [128, 128], BF16)
    onesn = din("onesn", [128, 65], BF16)
    onesf = din("onesf", [1, 128])
    trib = din("trib", [4, 128, 512], BF16)
    eselb = din("eselb", [8, E * 128], BF16)
    biasmask = din("biasmask", [128, NBIAS])
    kdidx = din("kdidx", [128, 4 * NP], U32)
    vdidx = din("vdidx", [128, 16], U32)

    out_own = nc.dram_tensor("out_own", [NTOK, C], F32, kind="ExternalOutput").ap()
    _ik = "ExternalOutput" if dbg else "Internal"
    k_dram = nc.dram_tensor("k_dram", [H * DH, T], BF16, kind=_ik).ap()
    v_dram = nc.dram_tensor("v_dram", [T, NP * 130], BF16, kind=_ik).ap()
    x1_dram = nc.dram_tensor("x1_dram", [NTOK, C], F32, kind=_ik).ap()
    den_dram = nc.dram_tensor("den_dram", [4, NP * 2 * 512], F32).ap()
    rb_dram = nc.dram_tensor("rb_dram", [4, NP * 2 * 512], BF16).ap()

    b = Builder()
    ctx = ExitStack()
    sem_names = ("pe", "ac", "dv", "st", "c0", "ast0", "ast1", "bc0",
                 "kq0", "kq1", "kq2", "kq3", "vq0", "vq1", "vq2", "vq3",
                 "vg0", "vg1", "kg0", "kg1", "wq0",
                 "wo0", "wr0", "wgu0", "wgu1", "wdl0", "wdl1", "x1s", "dfl",
                 "tap", "dn0", "dn1", "dr0", "dl0") + tuple(
                     f"lx{i}" for i in range(8))
    for s in sem_names:
        b.add_sem(s, ctx.enter_context(nc.semaphore(s)))

    def sb(name, shape, dt, side="left"):
        return ctx.enter_context(nc.sbuf_tensor(name, shape, dt, side=side))

    s_identf = sb("s_identf", [128, 128], F32)
    s_identb = sb("s_identb", [128, 128], BF16)
    s_onesb = sb("s_onesb", [128, 128], BF16)
    s_onesn = sb("s_onesn", [128, 65], BF16)
    s_onesf = sb("s_onesf", [1, 128], F32)
    s_temp = sb("s_temp", [1, 1], F32)
    s_xlnfm = sb("s_xlnfm", [128, 8 * NTOK], BF16)
    s_wfm = sb("s_wfm", [8, NTOK], BF16)
    s_mrowb = sb("s_mrowb", [1, 16], BF16)
    s_rt = sb("s_rt", [128, 1], F32)
    s_rt1 = sb("s_rt1", [1, 1], F32)

    rP = ExitStack()
    s_misc = rP.enter_context(nc.sbuf_tensor("s_misc", [128, 64], F32, side="right"))
    s_parts = rP.enter_context(nc.sbuf_tensor("s_parts", [128, 176], F32, side="right"))
    s_biasm = rP.enter_context(nc.sbuf_tensor("s_biasm", [128, NBIAS], F32, side="right"))
    s_mcol = rP.enter_context(nc.sbuf_tensor("s_mcol", [128, 16], F32, side="right"))

    def const_loads():
        pairs = [
            (s_identf[:], identf), (s_identb[:], identb), (s_onesb[:], onesb),
            (s_onesn[:], onesn), (s_onesf[:], onesf),
            (s_biasm[:], biasmask), (s_temp[:], temp),
        ]
        for d, s in pairs:
            b.op("sync", lambda E, d=d, s=s: E.dma_start(out=d, in_=s),
                 [("c0", 16)])
        return b.mark("c0")

    qkv_stack = ExitStack()
    s_wq = qkv_stack.enter_context(nc.sbuf_tensor("s_wq", [128, NP * 1024], BF16))
    s_wk = qkv_stack.enter_context(nc.sbuf_tensor("s_wk", [128, NP * 1024], BF16))
    s_wv = qkv_stack.enter_context(nc.sbuf_tensor("s_wv", [128, NP * 1024], BF16))

    def wslice(wsb, pr, cc):
        return wsb[:, pr * 1024 + cc * 128: pr * 1024 + (cc + 1) * 128]

    a_stack = ExitStack()

    def sbr(name, shape, dt):
        return a_stack.enter_context(nc.sbuf_tensor(name, shape, dt, side="right"))

    xbA = [sbr(f"xbA{i}", [128, 1024], F32) for i in range(8)]
    xlntmA = [sbr(f"xlntmA{i}", [128, 1024], BF16) for i in range(2)]
    statsA = sbr("statsA", [128, 12], F32)
    mvA = sbr("mvA", [128, 2], F32)
    rstdA = sbr("rstdA", [128, 1], F32)
    xlnfmA = [sbr(f"xlnfmA{s}", [128, 8 * 512], BF16) for s in range(2)]
    kstage = [sbr(f"kstage{i}", [128, NP * 512], BF16) for i in range(2)]
    vstage = [sbr(f"vstage{i}", [128, 4 * NP * 130], BF16)
              for i in range(2)]
    vfmt = [sbr(f"vfmt{i}", [128, 512], BF16) for i in range(2)]
    k2t = sbr("k2t", [128, 512], BF16)

    psA = ExitStack()

    def pA(name, shape, dt):
        return psA.enter_context(nc.psum_tensor(name, shape, dt))

    psA_tb = [pA(f"psA_tb{i}", [128, 1024], BF16) for i in range(2)]
    psA_k = [pA(f"psA_k{i}", [128, 512], F32) for i in range(2)]
    psA_v = [pA(f"psA_v{i}", [128, 512], F32) for i in range(2)]
    psA_vt = pA("psA_vt", [128, 512], BF16)
    psA_n = pA("psA_n", [128, 512], F32)

    consts_ld = const_loads()
    for e in ("tensor", "vector", "scalar", "gpsimd"):
        b.wait(e, "c0", consts_ld)

    for wsb, wsrc in ((s_wq, wqT), (s_wk, wkT), (s_wv, wvT)):
        d = wsb[:].rearrange("p (pr cc d) -> p pr cc d", pr=NP, cc=8)
        s = wsrc.rearrange("pr (cc p) d -> p pr cc d", p=128)
        b.op("gpsimd", lambda E, d=d, s=s: E.dma_start(out=d, in_=s),
             [("wq0", 16)])
    qkvw_ld = b.mark("wq0")
    b.wait("tensor", "wq0", qkvw_ld)

    for s2 in range(2):
        ap = vstage[s2][:].rearrange(
            "p (tt pr a n) -> p tt pr a n", tt=4, pr=NP, a=2)[:, :, :, :, 64:65]
        b.op("vector", lambda E, ap=ap: E.memset(ap, 1.0), [("dv", 1)])

    b.phase("pre")
    xmark = {}

    def load_x(tc):
        d = xbA[tc % 8]
        s = xfull[tc * 128:(tc + 1) * 128, :]
        lxs = f"lx{tc % 8}"
        b.op("sync", lambda E, d=d, s=s: E.dma_start(out=d[:], in_=s),
             [(lxs, 16)])
        xmark[tc] = (lxs, b.mark(lxs))

    for tc in range(8):
        load_x(tc)

    lnm = {}
    fm_evac = {}
    kevd = {}
    vevd = {}
    vftm = {}
    nredd = {}
    silm = {}
    stmark = {}
    fmd = {}

    def emit_tilesA(sc):
        for k in range(4):
            tc = 4 * sc + k
            xt = xbA[tc % 8]
            b.wait("vector", xmark[tc][0], xmark[tc][1])
            b.op("vector", lambda E, x=xt: E.bn_stats(statsA[:, 0:6], x[:, 0:512]),
                 [("dv", 1)])
            b.op("vector", lambda E, x=xt: E.bn_stats(statsA[:, 6:12],
                 x[:, 512:1024]), [("dv", 1)])
            b.bar("vector")
            b.op("vector", lambda E: E.bn_aggr(mvA[:], statsA[:, 0:12]),
                 [("dv", 1)])
            agg = b.mark("dv")
            b.wait("scalar", "dv", agg)
            b.op("scalar", lambda E: E.activation(rstdA[:], mvA[:, 1:2], AF.Ln,
                 bias=f32(1e-5)), [("ac", 1)])
            b.bar("scalar")
            b.op("scalar", lambda E: E.activation(rstdA[:], rstdA[:], AF.Exp,
                 scale=f32(-0.5)), [("ac", 1)])
            b.wait("vector", "ac", b.mark("ac"))
            b.bar("vector")
            xo = xlntmA[tc % 2]
            b.op("vector", lambda E, o=xo, x=xt: E.tensor_scalar(
                o[:], x[:], mvA[:, 0:1], rstdA[:, 0:1], ALU.subtract, ALU.mult),
                [("dv", 1)])
            lnm[tc] = b.mark("dv")
            if tc + 8 < 4 * NSC:
                b.wait("sync", "dv", lnm[tc])
                load_x(tc + 8)

            def emit_fm(tc2, sc=sc):
                k2 = tc2 % 4
                b.wait("tensor", "dv", lnm[tc2])
                if tc2 - 2 in fm_evac:
                    b.wait("tensor", "dv", fm_evac[tc2 - 2])
                xo2 = xlntmA[tc2 % 2]
                for cc in range(8):
                    o = psA_tb[k2 % 2][:, cc * 128:(cc + 1) * 128]
                    i = xo2[:, cc * 128:(cc + 1) * 128]
                    b.op("tensor", lambda E, o=o, i=i: E.transpose(
                        o, i, s_identb[:]), [("pe", 1)] if cc == 7 else ())
                trm = b.mark("pe")
                b.wait("vector", "pe", trm)
                d = xlnfmA[(tc2 // 4) % 2][:].rearrange(
                    "p (cc n) -> p cc n", cc=8)[:, :, k2 * 128:(k2 + 1) * 128]
                s = psA_tb[k2 % 2][:].rearrange("p (cc n) -> p cc n", cc=8)
                b.op("vector", lambda E, d=d, s=s: E.tensor_copy(d, s),
                     [("dv", 1)])
                fm_evac[tc2] = b.mark("dv")

            if k >= 1:
                emit_fm(tc - 1)
            if k == 3:
                emit_fm(tc)
        fmd[sc] = b.mark("dv")

    def emit_projA(sc):
        fm_done = fmd[sc]

        if sc >= 2:
            b.wait("tensor", f"ast{sc % 2}", stmark[sc - 2])
        b.wait("tensor", "dv", fm_done)
        for pr in range(NP):
            if (sc, pr - 2) in kevd:
                b.wait("tensor", "dv", kevd[(sc, pr - 2)])
            elif sc > 0 and (sc - 1, pr + NP - 2) in kevd:
                b.wait("tensor", "dv", kevd[(sc - 1, pr + NP - 2)])
            for cc in range(8):
                o = psA_k[pr % 2]
                l = wslice(s_wk, pr, cc)
                r = xlnfmA[sc % 2][:, cc * 512:(cc + 1) * 512]
                b.op("tensor", lambda E, o=o, l=l, r=r, cc=cc: E.matmul(
                    o[:, :], l, r, start=(cc == 0), stop=(cc == 7)),
                    [("pe", 1)] if cc == 7 else ())
            km = b.mark("pe")
            if (sc, pr - 2) in vevd:
                b.wait("tensor", "dv", vevd[(sc, pr - 2)])
            elif sc > 0 and (sc - 1, pr + NP - 2) in vevd:
                b.wait("tensor", "dv", vevd[(sc - 1, pr + NP - 2)])
            for cc in range(8):
                o = psA_v[pr % 2]
                l = wslice(s_wv, pr, cc)
                r = xlnfmA[sc % 2][:, cc * 512:(cc + 1) * 512]
                b.op("tensor", lambda E, o=o, l=l, r=r, cc=cc: E.matmul(
                    o[:, :], l, r, start=(cc == 0), stop=(cc == 7)),
                    [("pe", 1)] if cc == 7 else ())
            vm = b.mark("pe")
            b.wait("vector", "pe", km)
            d = kstage[sc % 2][:, pr * 512:(pr + 1) * 512]
            b.op("vector", lambda E, d=d, s=psA_k[pr % 2]: E.tensor_copy(
                d, s[:, :]), [("dv", 1)])
            kevd[(sc, pr)] = b.mark("dv")
            b.wait("vector", "pe", vm)
            b.op("vector", lambda E, d=vfmt[pr % 2], s=psA_v[pr % 2]:
                 E.tensor_copy(d[:, :], s[:, :]), [("dv", 1)])
            vftm[pr] = b.mark("dv")

            def emit_vtrans(pr2, sc=sc):
                b.wait("tensor", "dv", vftm[pr2])
                if (sc, pr2 - 1) in vevd:
                    b.wait("tensor", "dv", vevd[(sc, pr2 - 1)])
                elif sc > 0:
                    b.wait("tensor", "dv", vevd[(sc - 1, NP - 1)])
                for tt in range(4):
                    o = psA_vt[:, tt * 128:(tt + 1) * 128]
                    i = vfmt[pr2 % 2][:, tt * 128:(tt + 1) * 128]
                    b.op("tensor", lambda E, o=o, i=i: E.transpose(
                        o, i, s_identb[:]), [("pe", 1)] if tt == 3 else ())
                vtm = b.mark("pe")
                b.wait("vector", "pe", vtm)
                dst = vstage[sc % 2][:].rearrange(
                    "p (tt pr2 a n) -> p tt pr2 a n", tt=4, pr2=NP,
                    a=2)[:, :, pr2, :, 0:64]
                srcv = psA_vt[:].rearrange("p (tt a n) -> p tt a n",
                                           tt=4, a=2)
                b.op("vector", lambda E, d=dst, s=srcv: E.tensor_copy(d, s),
                     [("dv", 1)])
                vevd[(sc, pr2)] = b.mark("dv")

            if pr >= 1:
                emit_vtrans(pr - 1)
            if pr == NP - 1:
                emit_vtrans(pr)
            b.wait("scalar", "dv", kevd[(sc, pr)])
            if silm:
                b.wait("scalar", "pe", silm[max(silm)])
            s = kstage[sc % 2][:, pr * 512:(pr + 1) * 512]
            b.op("scalar", lambda E, s=s: E.activation(k2t[:], s, AF.Square),
                 [("ac", 1)])
            k2m = b.mark("ac")
            b.wait("tensor", "ac", k2m)
            if nredd:
                b.wait("tensor", "dv", nredd[max(nredd)])
            b.op("tensor", lambda E: E.matmul(psA_n[0:65, :], s_onesn[:], k2t[:],
                 start=True, stop=True), [("pe", 1)])
            silm[(sc, pr)] = b.mark("pe")
            b.wait("vector", "pe", silm[(sc, pr)])
            d = s_parts[0:65, pr * 16 + sc:pr * 16 + sc + 1]
            b.op("vector", lambda E, d=d: E.reduce_max(d, psA_n[0:65, :],
                 axis=AX.X), [("dv", 1)])
            nredd[(sc, pr)] = b.mark("dv")
        kv_evacs = b.mark("dv")
        b.wait("sync", "dv", kv_evacs)
        asts = f"ast{sc % 2}"
        dko = k_dram.rearrange("(pr p) t -> p pr t", p=128)[
            :, :, sc * 512:(sc + 1) * 512]
        sko = kstage[sc % 2][:].rearrange("p (pr n) -> p pr n", pr=NP)
        b.op("sync", lambda E, d=dko, s=sko: E.dma_start(out=d, in_=s),
             [(asts, 16)])
        for tt in range(4):
            d = v_dram[sc * 512 + tt * 128:sc * 512 + (tt + 1) * 128, :]
            s = vstage[sc % 2][:, tt * NP * 130:(tt + 1) * NP * 130]
            b.op("sync", lambda E, d=d, s=s: E.dma_start(out=d, in_=s),
                 [(asts, 16)])
        stmark[sc] = b.mark(asts)

    emit_tilesA(0)
    for sc in range(NSC):
        if sc + 1 < NSC:
            emit_tilesA(sc + 1)
        emit_projA(sc)
    kv_done_ast = (b.mark("ast0"), b.mark("ast1"))

    b.bar("vector")
    b.op("vector", lambda E: E.tensor_reduce(
        s_parts[0:65, 128:136],
        s_parts[0:65, 0:128].rearrange("p (pr s) -> p pr s", pr=NP),
        AX.X, ALU.max), [("dv", 1)])
    b.bar("vector")
    b.op("vector", lambda E: E.tensor_copy(s_misc[0:1, 0:8],
         s_parts[0:1, 128:136]), [("dv", 1)])
    b.op("vector", lambda E: E.tensor_copy(s_misc[0:1, 8:16],
         s_parts[64:65, 128:136]), [("dv", 1)])

    psA.close()
    a_stack.close()

    b.phase("A")
    r2 = ExitStack()

    def sbr2(name, shape, dt):
        return r2.enter_context(nc.sbuf_tensor(name, shape, dt, side="right"))

    x_tm = [sbr2(f"x_tm{i}", [128, 1024], F32) for i in range(4)]
    r3 = ExitStack()
    s_biasall = r3.enter_context(nc.sbuf_tensor(
        "s_biasall", [128, H * NBIAS], BF16, side="right"))
    q_fm = [r3.enter_context(nc.sbuf_tensor(f"q_fm{i}", [128, NTOK], BF16,
                                            side="right")) for i in range(NP)]

    ps2 = ExitStack()
    pstA2 = [ps2.enter_context(nc.psum_tensor(f"pstA2_{i}", [128, 1024], BF16))
             for i in range(2)]
    psqA2 = [ps2.enter_context(nc.psum_tensor(f"psqA2_{i}", [128, 512], F32))
             for i in range(2)]
    psnA2 = ps2.enter_context(nc.psum_tensor("psnA2", [128, 512], F32))

    a2_stack = ExitStack()

    def sbr4(name, shape, dt):
        return a2_stack.enter_context(
            nc.sbuf_tensor(name, shape, dt, side="right"))

    xlntm2 = [sbr4(f"xlntm2_{i}", [128, 1024], BF16) for i in range(2)]
    stats2 = sbr4("stats2", [128, 12], F32)
    mv2 = sbr4("mv2", [128, 2], F32)
    rstd2 = sbr4("rstd2", [128, 1], F32)
    q2t = sbr4("q2t", [128, 512], BF16)

    xm2 = {}

    def load_x2(ot):
        d = x_tm[ot % 4]
        s = xown[ot * 128:(ot + 1) * 128, :]
        lxs = f"lx{ot % 4}"
        b.op("sync", lambda E, d=d, s=s: E.dma_start(out=d[:], in_=s),
             [(lxs, 16)])
        xm2[ot] = (lxs, b.mark(lxs))

    b.wait("sync", "ast0", kv_done_ast[0])
    b.wait("sync", "ast1", kv_done_ast[1])
    for ot in range(4):
        load_x2(ot)
    fm2_evac = {}
    lnA2 = {}
    for ot in range(16):
        b.wait("vector", xm2[ot][0], xm2[ot][1])
        xt = x_tm[ot % 4]
        b.op("vector", lambda E, x=xt: E.bn_stats(stats2[:, 0:6], x[:, 0:512]),
             [("dv", 1)])
        b.op("vector", lambda E, x=xt: E.bn_stats(stats2[:, 6:12], x[:, 512:1024]),
             [("dv", 1)])
        b.bar("vector")
        b.op("vector", lambda E: E.bn_aggr(mv2[:], stats2[:, 0:12]), [("dv", 1)])
        b.wait("scalar", "dv", b.mark("dv"))
        b.op("scalar", lambda E: E.activation(rstd2[:], mv2[:, 1:2], AF.Ln,
             bias=f32(1e-5)), [("ac", 1)])
        b.bar("scalar")
        b.op("scalar", lambda E: E.activation(rstd2[:], rstd2[:], AF.Exp,
             scale=f32(-0.5)), [("ac", 1)])
        b.wait("vector", "ac", b.mark("ac"))
        b.bar("vector")
        xo = xlntm2[ot % 2]
        b.op("vector", lambda E, o=xo, x=xt: E.tensor_scalar(
            o[:], x[:], mv2[:, 0:1], rstd2[:, 0:1], ALU.subtract, ALU.mult),
            [("dv", 1)])
        ln2 = b.mark("dv")
        lnA2[ot] = ln2
        if ot + 4 < 16:
            b.wait("sync", "dv", ln2)
            load_x2(ot + 4)

        def emit_fm2(ot2):
            b.wait("tensor", "dv", lnA2[ot2])
            if ot2 - 2 in fm2_evac:
                b.wait("tensor", "dv", fm2_evac[ot2 - 2])
            xo2 = xlntm2[ot2 % 2]
            for cc in range(8):
                o = pstA2[ot2 % 2][:, cc * 128:(cc + 1) * 128]
                i = xo2[:, cc * 128:(cc + 1) * 128]
                b.op("tensor", lambda E, o=o, i=i: E.transpose(
                    o, i, s_identb[:]), [("pe", 1)] if cc == 7 else ())
            trm = b.mark("pe")
            b.wait("vector", "pe", trm)
            d = s_xlnfm[:].rearrange("p (cc n) -> p cc n", cc=8)[
                :, :, ot2 * 128:(ot2 + 1) * 128]
            s = pstA2[ot2 % 2][:].rearrange("p (cc n) -> p cc n", cc=8)
            b.op("vector", lambda E, d=d, s=s: E.tensor_copy(d, s),
                 [("dv", 1)])
            fm2_evac[ot2] = b.mark("dv")

        if ot >= 1:
            emit_fm2(ot - 1)
        if ot == 15:
            emit_fm2(15)
    fm2_done = b.mark("dv")
    b.wait("tensor", "dv", fm2_done)

    qevm = {}
    for pr in range(NP):
        for tt in range(4):
            key = pr * 4 + tt
            if key - 2 in qevm:
                b.wait("tensor", "dv", qevm[key - 2])
            for cc in range(8):
                o = psqA2[tt % 2]
                l = wslice(s_wq, pr, cc)
                r = s_xlnfm[:, cc * NTOK + tt * 512:
                            cc * NTOK + (tt + 1) * 512]
                b.op("tensor", lambda E, o=o, l=l, r=r, cc=cc: E.matmul(
                    o[:, :], l, r, start=(cc == 0), stop=(cc == 7)),
                    [("pe", 1)] if cc == 7 else ())
            qm = b.mark("pe")
            b.wait("vector", "pe", qm)
            d = q_fm[pr][:, tt * 512:(tt + 1) * 512]
            b.op("vector", lambda E, d=d, s=psqA2[tt % 2]: E.tensor_copy(
                d, s[:, :]), [("dv", 1)])
            qevm[key] = b.mark("dv")
    bias_done = b.mark("dv")

    tap1 = 0
    if dbg and dbg >= 2:
        dbg_misc = nc.dram_tensor("dbg_misc", [128, 64], F32,
                                  kind="ExternalOutput").ap()
        dbg_bias = nc.dram_tensor("dbg_bias", [128, H * NBIAS], BF16,
                                  kind="ExternalOutput").ap()
        dbg_qfm = nc.dram_tensor("dbg_qfm", [NP, 128, NTOK], BF16,
                                 kind="ExternalOutput").ap()
        dbg_xlnA2 = nc.dram_tensor("dbg_xlnA2", [8, 128, NTOK], BF16,
                                   kind="ExternalOutput").ap()
        b.wait("sync", "dv", bias_done)
        b.op("sync", lambda E: E.dma_start(out=dbg_misc, in_=s_misc[:]),
             [("tap", 16)])
        b.op("sync", lambda E: E.dma_start(out=dbg_bias, in_=s_biasall[:]),
             [("tap", 16)])
        for i in range(NP):
            b.op("sync", lambda E, i=i: E.dma_start(
                out=dbg_qfm[i], in_=q_fm[i][:]), [("tap", 16)])
        for i in range(8):
            b.op("sync", lambda E, i=i: E.dma_start(
                out=dbg_xlnA2[i],
                in_=s_xlnfm[:, i * NTOK:(i + 1) * NTOK]), [("tap", 16)])
        tap1 = b.mark("tap")

    a2_stack.close()
    ps2.close()
    qkv_stack.close()

    b.phase("A2")
    b_outer = ExitStack()
    b_stack = ExitStack()

    def sbl(name, shape, dt):
        return b_stack.enter_context(nc.sbuf_tensor(name, shape, dt))

    y_fm = [b_outer.enter_context(nc.sbuf_tensor(
        f"y_fm{i}", [128, NTOK], BF16)) for i in range(NP)]
    s_trib = sbl("s_trib", [128, 4 * 512], BF16)
    s_kdidx = sbl("s_kdidx", [128, 4 * NP], U32)
    s_vdidx = sbl("s_vdidx", [128, 16], U32)
    ksbB = [sbl(f"ksbB{i}", [128, 512], BF16) for i in range(4)]
    vsbB = [sbl(f"vsbB{i}", [128, 520], BF16) for i in range(4)]
    kdgB = [sbl(f"kdgB{i}", [128, 512], BF16) for i in range(2)]
    vdgB = [[sbl(f"vdgB{jj}_{i}", [128, NP * 130], BF16) for i in range(4)]
            for jj in range(2)]
    expB = [sbl(f"expB{i}", [128, 512], BF16) for i in range(4)]
    s2B = sbl("s2B", [65, 512], F32)
    rB = sbl("rB", [1, 512], F32)
    rbB = sbl("rbB", [1, 512], BF16)
    s_woB = sbl("s_woB", [128, NP * 1024], BF16)
    ofmC = [sbl(f"ofmC{i}", [128, 512], BF16) for i in range(2)]
    xlntmC = [sbl(f"xlntmC{i}", [128, 1024], BF16) for i in range(2)]
    statsC = sbl("statsC", [128, 12], F32)
    mvC = sbl("mvC", [128, 2], F32)
    rstdC = sbl("rstdC", [128, 1], F32)
    s_wrB = sbl("s_wrB", [128, 64], BF16)
    rn = {}
    for nm in ("zz", "ex1", "inh", "dist", "zhi", "zone", "w_", "fb2", "fbx",
               "topm", "fbw"):
        rn[nm] = sbl(f"r_{nm}", [128, 8], F32)
    for nm in ("wsum", "ne", "mx1", "mx2", "fbs", "rec"):
        rn[nm] = sbl(f"r_{nm}", [128, 1], F32)
    wtmB = sbl("wtmB", [128, 8], BF16)
    wtmAll = sbl("wtmAll", [128, 16 * 8], BF16)
    s_wrF = sbl("s_wrF", [128, 64], F32)
    den_st = [sbl(f"den_st{i}", [1, 512], F32) for i in range(2)]
    denT = sbl("denT", [128, 64], F32)
    denR = sbl("denR", [128, 64], BF16)
    rbAll = sbl("rbAll", [1, NP * 2 * 512], BF16)
    ps3 = ExitStack()
    pssB = [ps3.enter_context(nc.psum_tensor(f"pssB{i}", [128, 512], F32))
            for i in range(4)]
    psavB = [ps3.enter_context(nc.psum_tensor(f"psavB{i}", [128, 512], F32))
             for i in range(2)]
    psoC = ps3.enter_context(nc.psum_tensor("psoC", [128, 512], F32))
    pscB = ps3.enter_context(nc.psum_tensor("pscB", [128, 1024], BF16))

    for dd, ss in ((s_trib[:].rearrange("p (t n) -> p t n", t=4),
                    trib.rearrange("t p n -> p t n")),
                   (s_kdidx[:], kdidx), (s_vdidx[:], vdidx)):
        b.op("sync", lambda E, d=dd, s=ss: E.dma_start(out=d, in_=s),
             [("bc0", 16)])
    bconst_ld = b.mark("bc0")
    b.wait("gpsimd", "bc0", bconst_ld)
    b.wait("vector", "bc0", bconst_ld)
    for qn in ("gpsimd", "sync"):
        b.wait(qn, "ast0", kv_done_ast[0])
        b.wait(qn, "ast1", kv_done_ast[1])
    b.wait("scalar", "dv", bias_done)
    b.wait("tensor", "dv", bias_done)

    GC = float(1.0 / math.e)
    GL = float(0.5 - math.log(4.0 / 3.0))
    GU = 0.5
    if dbg and dbg >= 3:
        dbg_rt = nc.dram_tensor("dbg_rt", [16, 128, 96], F32,
                                kind="ExternalOutput").ap()
        dbg_wrB = nc.dram_tensor("dbg_wrB", [128, 64], BF16,
                                 kind="ExternalOutput").ap()
        dbg_srt = nc.dram_tensor("dbg_srt", [128, 1], F32,
                                 kind="ExternalOutput").ap()
    b.op("vector", lambda E: E.reciprocal(s_rt1[:], s_temp[:]), [("dv", 1)])
    b.op("sync", lambda E: E.dma_start(out=s_wrF[:], in_=wr), [("wr0", 16)])
    b.wait("vector", "wr0", b.mark("wr0"))
    b.bar("vector")
    b.op("vector", lambda E: E.tensor_copy(s_wrB[:], s_wrF[:]), [("dv", 1)])
    wrm = b.mark("dv")
    b.wait("tensor", "dv", wrm)
    b.op("tensor", lambda E: E.matmul(psoC[:, 0:1], s_onesf[:, :], s_rt1[:],
         start=True, stop=True), [("pe", 1)])
    b.wait("vector", "pe", b.mark("pe"))
    b.op("vector", lambda E: E.tensor_copy(s_rt[:], psoC[:, 0:1]), [("dv", 1)])
    rtm = b.mark("dv")
    if dbg and dbg >= 3:
        b.wait("sync", "dv", max(rtm, wrm))
        b.op("sync", lambda E: E.dma_start(out=dbg_wrB, in_=s_wrB[:]),
             [("tap", 16)])
        b.op("sync", lambda E: E.dma_start(out=dbg_srt, in_=s_rt[:]),
             [("tap", 16)])

    psoC_last = [rtm]
    pscB_last = [0]
    ofm_free = [0, 0]
    slot_tile_free = {}
    slot_lxm = {}
    wfm_ev = {}
    xln2ev = {}
    pending_C = []

    def piece_norm_pre(j):
        def emit():
            dns = f"dn{j % 2}"
            b.wait("sync", dns, b.mark(dns))
            if denT_free[0]:
                b.wait("sync", "dv", denT_free[0])
            s = den_dram[j].rearrange("(p n) -> p n", p=128)
            b.op("sync", lambda E, s=s: E.dma_start(out=denT[:], in_=s),
                 [("dr0", 16)])
            ldm = b.mark("dr0")
            b.wait("vector", "dr0", ldm)
            b.bar("vector")
            b.op("vector", lambda E: E.reciprocal(denT[:], denT[:]),
                 [("dv", 1)])
            b.bar("vector")
            b.op("vector", lambda E: E.tensor_copy(denR[:], denT[:]),
                 [("dv", 1)])
            rm = b.mark("dv")
            denT_free[0] = rm
            b.wait("sync", "dv", rm)
            dd = rb_dram[j].rearrange("(p n) -> p n", p=128)
            b.op("sync", lambda E, d=dd: E.dma_start(out=d, in_=denR[:]),
                 [("dr0", 16)])
            b.wait("sync", "dr0", b.mark("dr0"))
            if (j - 1, NP - 1) in norm_bm:
                b.wait("sync", "pe", norm_bm[(j - 1, NP - 1)])
            b.op("sync", lambda E: E.dma_start(out=rbAll[:, :],
                 in_=rb_dram[j]), [("dl0", 16)])
            rbAll_ld[0] = b.mark("dl0")
        return emit

    def piece_norm(j, pr):
        def emit():
            b.wait("tensor", "dl0", rbAll_ld[0])
            b.wait("tensor", "dv", psoC_last[0])
            for half in (0, 1):
                r = rbAll[0:1, (pr * 2 + half) * 512:
                          (pr * 2 + half + 1) * 512]
                b.op("tensor", lambda E, r=r, half=half: E.matmul(
                    psoC[half * 64:half * 64 + 64, :],
                    s_onesb[0:1, 0:64], r,
                    start=True, stop=True), [("pe", 1)])
            bm = b.mark("pe")
            norm_bm[(j, pr)] = bm
            b.wait("vector", "pe", bm)
            for half in (0, 1):
                d = y_fm[pr][half * 64:half * 64 + 64,
                             j * 512:(j + 1) * 512]
                b.op("vector", lambda E, d=d, half=half: E.tensor_tensor(
                    d, d, psoC[half * 64:half * 64 + 64, :], ALU.mult),
                    [("dv", 1)])
            psoC_last[0] = b.mark("dv")
        return emit

    def piece_loads(j):
        def emit():
            for tb in range(4):
                ot = 4 * j + tb
                if j == 0:
                    b.wait("sync", "dv", lnA2[12 + tb])
                else:
                    fdv, fst = slot_tile_free[tb]
                    b.wait("sync", "dv", fdv)
                    b.wait("sync", "x1s", fst)
                d = x_tm[tb]
                s = xown[ot * 128:(ot + 1) * 128, :]
                lxs = f"lx{tb}"
                b.op("sync", lambda E, d=d, s=s: E.dma_start(
                    out=d[:], in_=s), [(lxs, 16)])
                slot_lxm[tb] = (lxs, b.mark(lxs))
        return emit

    def piece_outproj(j, cc, ydone):
        def emit():
            b.wait("tensor", "wo0", 16)
            b.wait("tensor", "dv", psoC_last[0])
            for pr in range(NP):
                l = s_woB[:, pr * 1024 + cc * 128:pr * 1024 + (cc + 1) * 128]
                r = y_fm[pr][:, j * 512:(j + 1) * 512]
                b.op("tensor", lambda E, l=l, r=r, pr=pr: E.matmul(
                    psoC[:, :], l, r, start=(pr == 0), stop=(pr == NP - 1)),
                    [("pe", 1)] if pr == NP - 1 else ())
            om = b.mark("pe")
            b.wait("vector", "pe", om)
            if ofm_free[cc % 2]:
                b.wait("vector", "pe", ofm_free[cc % 2])
            b.op("vector", lambda E, d=ofmC[cc % 2]: E.tensor_copy(
                d[:, :], psoC[:, :]), [("dv", 1)])
            ev = b.mark("dv")
            psoC_last[0] = ev
            b.wait("tensor", "dv", ev)
            b.wait("tensor", "dv", pscB_last[0])
            for tb in range(4):
                o = pscB[:, tb * 128:(tb + 1) * 128]
                i = ofmC[cc % 2][:, tb * 128:(tb + 1) * 128]
                b.op("tensor", lambda E, o=o, i=i: E.transpose(
                    o, i, s_identb[:]), [("pe", 1)] if tb == 3 else ())
            trm = b.mark("pe")
            ofm_free[cc % 2] = trm
            b.wait("vector", "pe", trm)
            if cc == 0:
                for tb in range(4):
                    b.wait("vector", slot_lxm[tb][0], slot_lxm[tb][1])
            for tb in range(4):
                xap = x_tm[tb][:, cc * 128:(cc + 1) * 128]
                pap = pscB[:, tb * 128:(tb + 1) * 128]
                b.op("vector", lambda E, x=xap, p=pap: E.tensor_tensor(
                    x, x, p, ALU.add), [("dv", 1)] if tb == 3 else ())
            pscB_last[0] = b.mark("dv")
        return emit

    def piece_tile(j, tb):
        def emit():
            ot = 4 * j + tb
            xt = x_tm[tb]
            b.bar("vector")
            b.op("vector", lambda E, x=xt: E.bn_stats(
                statsC[:, 0:6], x[:, 0:512]), [("dv", 1)])
            b.op("vector", lambda E, x=xt: E.bn_stats(
                statsC[:, 6:12], x[:, 512:1024]), [("dv", 1)])
            b.bar("vector")
            b.op("vector", lambda E: E.bn_aggr(mvC[:], statsC[:, 0:12]),
                 [("dv", 1)])
            b.wait("scalar", "dv", b.mark("dv"))
            b.op("scalar", lambda E: E.activation(rstdC[:], mvC[:, 1:2],
                 AF.Ln, bias=f32(1e-5)), [("ac", 1)])
            b.bar("scalar")
            b.op("scalar", lambda E: E.activation(rstdC[:], rstdC[:], AF.Exp,
                 scale=f32(-0.5)), [("ac", 1)])
            b.wait("vector", "ac", b.mark("ac"))
            b.bar("vector")
            xo = xlntmC[ot % 2]
            b.op("vector", lambda E, o=xo, x=xt: E.tensor_scalar(
                o[:], x[:], mvC[:, 0:1], rstdC[:, 0:1],
                ALU.subtract, ALU.mult), [("dv", 1)])
            ln2 = b.mark("dv")
            b.wait("sync", "dv", ln2)
            dx1 = x1_dram[ot * 128:(ot + 1) * 128, :]
            b.op("sync", lambda E, d=dx1, s=xt: E.dma_start(
                out=d, in_=s[:]), [("x1s", 16)])
            stm = b.mark("x1s")
            slot_tile_free[tb] = (ln2, stm)
            b.wait("tensor", "dv", ln2)
            b.wait("tensor", "dv", pscB_last[0])
            for cc in range(8):
                o = pscB[:, cc * 128:(cc + 1) * 128]
                i = xo[:, cc * 128:(cc + 1) * 128]
                b.op("tensor", lambda E, o=o, i=i: E.transpose(
                    o, i, s_identb[:]), [("pe", 1)] if cc == 7 else ())
            trm = b.mark("pe")
            b.wait("vector", "pe", trm)
            dfm = s_xlnfm[:].rearrange("p (cc n) -> p cc n", cc=8)[
                :, :, ot * 128:(ot + 1) * 128]
            sfm = pscB[:].rearrange("p (cc n) -> p cc n", cc=8)
            b.op("vector", lambda E, d=dfm, s=sfm: E.tensor_copy(d, s),
                 [("dv", 1)])
            xle = b.mark("dv")
            xln2ev[ot] = xle
            pscB_last[0] = xle
            b.wait("tensor", "dv", psoC_last[0])
            b.wait("tensor", "dv", b.mark("dv"))
            b.wait("tensor", "ac", b.mark("ac"))
            b.wait("vector", "pe", b.mark("pe"))
            b.wait("scalar", "dv", b.mark("dv"))
            b.wait("scalar", "pe", b.mark("pe"))
            for cc in range(8):
                o = psoC[:, 0:8]
                l = s_xlnfm[:, cc * NTOK + ot * 128:cc * NTOK + (ot + 1) * 128]
                r = s_wrB[:, cc * 8:(cc + 1) * 8]
                b.op("tensor", lambda E, o=o, l=l, r=r, cc=cc: E.matmul(
                    o, l, r, start=(cc == 0), stop=(cc == 7)),
                    [("pe", 1)] if cc == 7 else ())
            zm = b.mark("pe")
            b.wait("vector", "pe", zm)
            V = "vector"
            b.bar(V)
            b.op(V, lambda E: E.tensor_scalar(rn["zz"][:], psoC[:, 0:8],
                 s_rt[:, 0:1], None, ALU.mult), [("dv", 1)])
            zzm = b.mark("dv")
            psoC_last[0] = zzm
            b.wait("scalar", "dv", zzm)
            b.op("scalar", lambda E: E.activation(rn["ex1"][:], rn["zz"][:],
                 AF.Exp, scale=f32(-1.0)), [("ac", 1)])
            b.wait(V, "ac", b.mark("ac"))
            b.bar(V)
            b.op(V, lambda E: E.tensor_scalar(rn["inh"][:], rn["ex1"][:], 1.0,
                 None, ALU.add), [("dv", 1)])
            b.bar(V)
            b.op(V, lambda E: E.reciprocal(rn["inh"][:], rn["inh"][:]),
                 [("dv", 1)])
            b.bar(V)
            b.op(V, lambda E: E.tensor_scalar(rn["dist"][:], rn["inh"][:], GC,
                 None, ALU.subtract), [("dv", 1)])
            b.bar(V)
            b.op(V, lambda E: E.scalar_tensor_tensor(rn["dist"][:],
                 rn["dist"][:], -1.0, rn["dist"][:], ALU.mult, ALU.max),
                 [("dv", 1)])
            b.bar(V)
            b.op(V, lambda E: E.tensor_scalar(rn["zhi"][:], rn["inh"][:], GU,
                 None, ALU.is_le), [("dv", 1)])
            b.bar(V)
            b.op(V, lambda E: E.scalar_tensor_tensor(rn["zone"][:],
                 rn["inh"][:], GL, rn["zhi"][:], ALU.is_ge, ALU.mult),
                 [("dv", 1)])
            dm = b.mark("dv")
            b.wait("scalar", "dv", dm)
            b.op("scalar", lambda E: E.activation(rn["w_"][:], rn["dist"][:],
                 AF.Exp, scale=f32(-10.0)), [("ac", 1)])
            b.op("scalar", lambda E: E.activation(rn["fb2"][:], rn["dist"][:],
                 AF.Exp, scale=f32(-1.0 / 0.3)), [("ac", 1)])
            b.wait(V, "ac", b.mark("ac"))
            b.bar(V)
            b.op(V, lambda E: E.tensor_tensor(rn["w_"][:], rn["w_"][:],
                 rn["zone"][:], ALU.mult), [("dv", 1)])
            b.bar(V)
            b.op(V, lambda E: E.tensor_reduce(rn["wsum"][:], rn["w_"][:], AX.X,
                 ALU.add), [("dv", 1)])
            b.bar(V)
            b.op(V, lambda E: E.tensor_scalar(rn["ne"][:], rn["wsum"][:], 1e-8,
                 None, ALU.is_lt), [("dv", 1)])
            b.bar(V)
            b.op(V, lambda E: E.reduce_max(rn["mx1"][:], rn["fb2"][:],
                 axis=AX.X), [("dv", 1)])
            b.bar(V)
            b.op(V, lambda E: E.tensor_scalar(rn["fbx"][:], rn["fb2"][:],
                 rn["mx1"][:, 0:1], None, ALU.is_equal), [("dv", 1)])
            b.bar(V)
            b.op(V, lambda E: E.scalar_tensor_tensor(rn["fbx"][:], rn["fbx"][:],
                 -1e30, rn["fb2"][:], ALU.mult, ALU.add), [("dv", 1)])
            b.bar(V)
            b.op(V, lambda E: E.reduce_max(rn["mx2"][:], rn["fbx"][:],
                 axis=AX.X), [("dv", 1)])
            b.bar(V)
            b.op(V, lambda E: E.tensor_scalar(rn["topm"][:], rn["fb2"][:],
                 rn["mx2"][:, 0:1], None, ALU.is_ge), [("dv", 1)])
            b.bar(V)
            b.op(V, lambda E: E.tensor_tensor(rn["fbw"][:], rn["fb2"][:],
                 rn["topm"][:], ALU.mult), [("dv", 1)])
            b.bar(V)
            b.op(V, lambda E: E.tensor_reduce(rn["fbs"][:], rn["fbw"][:], AX.X,
                 ALU.add), [("dv", 1)])
            b.bar(V)
            b.op(V, lambda E: E.tensor_scalar(rn["fbs"][:], rn["fbs"][:], 1e-8,
                 None, ALU.max), [("dv", 1)])
            b.bar(V)
            b.op(V, lambda E: E.reciprocal(rn["rec"][:], rn["fbs"][:]),
                 [("dv", 1)])
            b.bar(V)
            b.op(V, lambda E: E.tensor_scalar(rn["fbw"][:], rn["fbw"][:],
                 rn["rec"][:, 0:1], None, ALU.mult), [("dv", 1)])
            b.bar(V)
            b.op(V, lambda E: E.tensor_tensor(rn["fbx"][:], rn["fbw"][:],
                 rn["w_"][:], ALU.subtract), [("dv", 1)])
            b.bar(V)
            b.op(V, lambda E: E.scalar_tensor_tensor(rn["w_"][:], rn["fbx"][:],
                 rn["ne"][:, 0:1], rn["w_"][:], ALU.mult, ALU.add),
                 [("dv", 1)])
            b.bar(V)
            b.op(V, lambda E: E.tensor_reduce(rn["wsum"][:], rn["w_"][:], AX.X,
                 ALU.add), [("dv", 1)])
            b.bar(V)
            b.op(V, lambda E: E.tensor_scalar(rn["wsum"][:], rn["wsum"][:],
                 1e-8, None, ALU.max), [("dv", 1)])
            b.bar(V)
            b.op(V, lambda E: E.reciprocal(rn["rec"][:], rn["wsum"][:]),
                 [("dv", 1)])
            b.bar(V)
            b.op(V, lambda E: E.tensor_scalar(wtmB[:], rn["w_"][:],
                 rn["rec"][:, 0:1], None, ALU.mult), [("dv", 1)])
            wm = b.mark("dv")
            if dbg and dbg >= 3:
                b.wait("sync", "dv", wm)
                flds = ["zz", "ex1", "inh", "dist", "zhi", "zone", "w_", "fb2",
                        "fbx", "topm", "fbw"]
                for fi, nm in enumerate(flds):
                    dd = dbg_rt[ot, :, fi * 8:(fi + 1) * 8]
                    b.op("sync", lambda E, d=dd, s=rn[nm]: E.dma_start(
                        out=d, in_=s[:]), [("tap", 16)])
                b.wait("vector", "tap", b.mark("tap"))
            b.bar(V)
            b.op(V, lambda E, ot=ot: E.tensor_copy(
                wtmAll[:, ot * 8:(ot + 1) * 8], wtmB[:]), [("dv", 1)])
            wfm_ev[ot] = b.mark("dv")
        return emit

    vdg_free = {0: 0, 1: 0}
    av_free = [0] * 2
    den_free = {(p, h): 0 for p in range(2) for h in range(2)}
    denT_free = [0]
    rbAll_ld = [0]
    norm_bm = {}
    pending_epi = None
    kdg_free = [0, 0]
    prev_pair_av = 0
    prev2_pair_av = 0
    prev3_pair_av = 0
    prev_pair_emark = 0
    pair_idx = 0
    kdram_flat = k_dram.rearrange("r (q c) -> (r q) c", c=512)
    for j in range(4):
        nq = SLOTQ[j]
        ncnk = nq + 4
        nquad = nq // 4
        if vdg_free[j % 2]:
            b.wait("gpsimd", "pe", vdg_free[j % 2])
        vgs = f"vg{j % 2}"
        for t in range(4):
            idx = s_vdidx[:, j * 4 + t:j * 4 + t + 1]
            b.op("gpsimd", lambda E, t=t, idx=idx, jj=j % 2:
                 E.indirect_dma_start(
                out=vdgB[jj][t][:], out_offset=None, in_=v_dram,
                in_offset=bass.IndirectOffsetOnAxis(ap=idx, axis=0)),
                [(vgs, 16)])
        vdgm = (vgs, b.mark(vgs))
        for pr in range(NP):
            if kdg_free[pr % 2]:
                b.wait("gpsimd", "pe", kdg_free[pr % 2])
            idx = s_kdidx[:, j * NP + pr:j * NP + pr + 1]
            kgs = f"kg{pr % 2}"
            b.op("gpsimd", lambda E, pr=pr, idx=idx: E.indirect_dma_start(
                out=kdgB[pr % 2][:], out_offset=None, in_=kdram_flat,
                in_offset=bass.IndirectOffsetOnAxis(ap=idx, axis=0)),
                [(kgs, 16)])
            kdgm = (kgs, b.mark(kgs))
            qmark = {}

            pp = pair_idx % 2

            def load_quad(q, pr=pr, pp=pp):
                bi = pp * 2 + q % 2
                dK = ksbB[bi][:]
                sK = k_dram[pr * 128:(pr + 1) * 128, q * 512:(q + 1) * 512]
                kqs = f"kq{bi}"
                vqs = f"vq{bi}"
                b.op("sync", lambda E, d=dK, s=sK: E.dma_start(out=d, in_=s),
                     [(kqs, 16)])
                dV = vsbB[bi][:].rearrange("p (c w) -> p c w", c=4)
                sV = v_dram[q * 512:(q + 1) * 512,
                            pr * 130:(pr + 1) * 130].rearrange(
                                "(c p) w -> p c w", p=128)
                b.op("gpsimd", lambda E, d=dV, s=sV: E.dma_start(out=d, in_=s),
                     [(vqs, 16)])
                qmark[q] = (kqs, b.mark(kqs), vqs, b.mark(vqs))

            if prev2_pair_av:
                b.wait("sync", "pe", prev2_pair_av)
                b.wait("gpsimd", "pe", prev2_pair_av)
            for q in range(min(2, nquad)):
                load_quad(q)
            h1, h2 = 2 * pr, 2 * pr + 1
            qc = slice(j * 512, (j + 1) * 512)
            smark, emark, avm, trimark = {}, {}, {}, {}

            def emit_qk(c, pr=pr, j=j, pp=pp):
                diag = c >= nq
                t = c - nq
                par = c % 2
                if c >= 2:
                    b.wait("tensor", "ac", emark[c - 2])
                else:
                    b.wait("tensor", "ac", prev_pair_emark)
                if diag:
                    b.wait("tensor", kdgm[0], kdgm[1])
                    kl0 = kdgB[pr % 2][0:64, t * 128:(t + 1) * 128]
                    kl1 = kdgB[pr % 2][64:128, t * 128:(t + 1) * 128]
                else:
                    q4 = c // 4
                    if c % 4 == 0:
                        b.wait("tensor", qmark[q4][0], qmark[q4][1])
                        b.wait("tensor", qmark[q4][2], qmark[q4][3])
                    kb = ksbB[pp * 2 + q4 % 2]
                    kl0 = kb[0:64, (c % 4) * 128:(c % 4 + 1) * 128]
                    kl1 = kb[64:128, (c % 4) * 128:(c % 4 + 1) * 128]
                r0 = q_fm[pr][0:64, qc]
                r1 = q_fm[pr][64:128, qc]
                b.op("tensor", lambda E, o=pssB[0 + par], l=kl0, r=r0: E.matmul(
                    o[:, :], l, r, start=True, stop=True))
                b.op("tensor", lambda E, o=pssB[2 + par], l=kl1, r=r1: E.matmul(
                    o[:, :], l, r, start=True, stop=True), [("pe", 1)])
                smark[c] = b.mark("pe")
                b.wait("scalar", "pe", smark[c])
                if c - 2 in avm:
                    b.wait("scalar", "pe", avm[c - 2])
                elif prev_pair_av:
                    b.wait("scalar", "pe", prev_pair_av)
                for half in (0, 1):
                    hh = h1 if half == 0 else h2
                    colb = (144 + t) if diag else (OFFJ[j] + c)
                    eb = expB[2 * par + half]
                    bias_ap = s_biasm[:, colb:colb + 1]
                    src = pssB[(0 + par) if half == 0 else (2 + par)]
                    b.op("scalar", lambda E, o=eb, s=src, ba=bias_ap:
                         E.activation(o[:], s[:, :], AF.Exp, bias=ba,
                                      scale=f32(0.125)), [("ac", 1)])
                emark[c] = b.mark("ac")
                if diag:
                    b.wait("vector", "ac", emark[c])
                    for half in (0, 1):
                        eb = expB[2 * par + half]
                        tm = s_trib[:, t * 512:(t + 1) * 512]
                        b.op("vector", lambda E, eb=eb, tm=tm: E.tensor_tensor(
                            eb[:], eb[:], tm, ALU.mult), [("dv", 1)])
                    trimark[c] = b.mark("dv")

            def emit_av(c, pr=pr, j=j, pp=pp):
                diag = c >= nq
                t = c - nq
                par = c % 2
                if diag:
                    b.wait("tensor", "dv", trimark[c])
                    if t == 0:
                        b.wait("tensor", vdgm[0], vdgm[1])
                else:
                    b.wait("tensor", "ac", emark[c])
                if c == 0:
                    for fm in av_free:
                        b.wait("tensor", "dv", fm)
                first, last = (c == 0), (c == ncnk - 1)
                for half in (0, 1):
                    eb = expB[2 * par + half]
                    if diag:
                        vl = vdgB[j % 2][t][:, pr * 130 + half * 65:
                                            pr * 130 + half * 65 + 65]
                    else:
                        vl = vsbB[pp * 2 + (c // 4) % 2][
                            :, (c % 4) * 130 + half * 65:
                            (c % 4) * 130 + half * 65 + 65]
                    o = psavB[half]
                    b.op("tensor", lambda E, o=o, vl=vl, es=eb,
                         first=first, last=last: E.matmul(
                             o[0:65, :], vl, es[:, :], start=first, stop=last),
                         [("pe", 1)] if half == 1 else ())
                avm[c] = b.mark("pe")
                if (not diag) and (c % 4 == 3):
                    q4 = c // 4
                    if q4 + 2 <= nquad - 1:
                        b.wait("sync", "pe", avm[c])
                        b.wait("gpsimd", "pe", avm[c])
                        load_quad(q4 + 2)

            emit_qk(0)
            emit_qk(1)
            if pending_epi is not None:
                pending_epi()
                pending_epi = None
            for _ in range(3):
                if pending_C:
                    pending_C.pop(0)()
            emit_av(0)
            for c in range(2, ncnk):
                emit_qk(c)
                emit_av(c - 1)
            emit_av(ncnk - 1)
            av_done = avm[ncnk - 1]
            kdg_free[pr % 2] = av_done
            def make_epilogue(pr=pr, qc=qc, av_done=av_done, j=j):
                def emit_epilogue():
                    dns = f"dn{j % 2}"
                    b.wait("vector", "pe", av_done)
                    for half in (0, 1):
                        b.bar("vector")
                        d = y_fm[pr][half * 64:half * 64 + 64, qc]
                        b.op("vector", lambda E, d=d, s=psavB[half]:
                             E.tensor_copy(d, s[0:64, :]), [("dv", 1)])
                        if den_free[(j % 2, half)]:
                            b.wait("vector", dns, den_free[(j % 2, half)])
                        b.bar("vector")
                        b.op("vector", lambda E, o=den_st[half],
                             s=psavB[half]: E.tensor_copy(
                            o[:, :], s[64:65, :]), [("dv", 1)])
                        cm = b.mark("dv")
                        av_free[half] = cm
                        b.wait("sync", "dv", cm)
                        dd = den_dram[j, (pr * 2 + half) * 512:
                                      (pr * 2 + half + 1) * 512]
                        b.op("sync", lambda E, d=dd, s=den_st[half]:
                             E.dma_start(out=d, in_=s[:]), [(dns, 16)])
                        den_free[(j % 2, half)] = b.mark(dns)
                return emit_epilogue
            pending_epi = make_epilogue()
            prev2_pair_av = prev_pair_av
            prev_pair_av = av_done
            prev_pair_emark = b.mark("ac")
            pair_idx += 1
        if pending_epi is not None:
            pending_epi()
            pending_epi = None
        slot_ydone = b.mark("dv")
        if j == 0:
            dwo = s_woB[:].rearrange("p (pr c) -> p pr c", pr=NP)
            swo = wo.rearrange("(pr p) c -> p pr c", p=128)
            b.op("gpsimd", lambda E, d=dwo, s=swo: E.dma_start(out=d, in_=s),
                 [("wo0", 16)])
        pending_C.append(piece_norm_pre(j))
        for pr2 in range(NP):
            pending_C.append(piece_norm(j, pr2))
        pending_C.append(piece_loads(j))
        for cc in range(8):
            pending_C.append(piece_outproj(j, cc, None))
        for tb in range(4):
            pending_C.append(piece_tile(j, tb))
        vdg_free[j % 2] = b.mark("pe")
    if pending_epi is not None:
        pending_epi()
        pending_epi = None
    while pending_C:
        pending_C.pop(0)()
    b.wait("tensor", "dv", b.mark("dv"))
    wtr = {}
    for ot in range(16):
        if ot >= 2:
            b.wait("tensor", "dv", wtr[ot - 2])
        o = pscB[0:8, (ot % 2) * 128:(ot % 2) * 128 + 128]
        b.op("tensor", lambda E, o=o, ot=ot: E.transpose(
            o, wtmAll[:, ot * 8:(ot + 1) * 8], s_identb[:]), [("pe", 1)])
        b.wait("vector", "pe", b.mark("pe"))
        dfm2 = s_wfm[0:8, ot * 128:(ot + 1) * 128]
        b.op("vector", lambda E, d=dfm2, o=o: E.tensor_copy(d, o),
             [("dv", 1)])
        wtr[ot] = b.mark("dv")
    y_done = b.mark("dv")
    wfm_done = y_done
    x1_stored = b.mark("x1s")
    tap2 = 0
    if dbg and dbg >= 2:
        dbg_yfm = nc.dram_tensor("dbg_yfm", [NP, 128, NTOK], BF16,
                                 kind="ExternalOutput").ap()
        dbg_wfm = nc.dram_tensor("dbg_wfm", [8, NTOK], BF16,
                                 kind="ExternalOutput").ap()
        dbg_xlnC = nc.dram_tensor("dbg_xlnC", [8, 128, NTOK], BF16,
                                  kind="ExternalOutput").ap()
        b.wait("sync", "dv", wfm_done)
        for i in range(NP):
            b.op("sync", lambda E, i=i: E.dma_start(
                out=dbg_yfm[i], in_=y_fm[i][:]), [("tap", 16)])
        b.op("sync", lambda E: E.dma_start(out=dbg_wfm, in_=s_wfm[0:8, :]),
             [("tap", 16)])
        for i in range(8):
            b.op("sync", lambda E, i=i: E.dma_start(
                out=dbg_xlnC[i],
                in_=s_xlnfm[:, i * NTOK:(i + 1) * NTOK]), [("tap", 16)])
        tap2 = b.mark("tap")
    ps3.close()
    b_stack.close()
    b_outer.close()
    r3.close()
    r2.close()
    rP.close()

    b.phase("C2")
    d_stack = ExitStack()

    def sbd(name, shape, dt):
        return d_stack.enter_context(nc.sbuf_tensor(name, shape, dt))

    s_eselb = sbd("s_eselb", [8, E * 128], BF16)
    wgbD = [sbd(f"wgbD{i}", [128, 8192], BF16) for i in range(2)]
    wubD = [sbd(f"wubD{i}", [128, 8192], BF16) for i in range(2)]
    wdbD = sbd("wdbD", [128, 8192], BF16)
    hsbD = [sbd(f"hsbD{i}", [128, 512], BF16) for i in range(8)]
    pswS = [sbd(f"pswS{i}", [128, 512], BF16) for i in range(2)]
    accsD = [[sbd(f"accsD{t}_{i}", [128, 512], F32) for i in range(8)]
             for t in range(4)]
    x1tD = sbd("x1tD", [128, 1024], F32)
    outtD = sbd("outtD", [128, 1024], F32)
    ps5 = ExitStack()
    pswD = ps5.enter_context(nc.psum_tensor("pswD", [128, 512], F32))
    psgD = [ps5.enter_context(nc.psum_tensor(f"psgD{i}", [128, 512], F32))
            for i in range(2)]
    psuD = [ps5.enter_context(nc.psum_tensor(f"psuD{i}", [128, 512], F32))
            for i in range(2)]
    psdD = [ps5.enter_context(nc.psum_tensor(f"psdD{i}", [128, 512], F32))
            for i in range(2)]
    pstrD = ps5.enter_context(nc.psum_tensor("pstrD", [128, 512], F32))

    def ldw(e, which, buf):
        src = {"g": wg, "u": wu, "d": wd}[which]
        sem = (f"wdl{e % 2}" if which == "d" else f"wgu{e % 2}")
        dap = buf[:].rearrange("p (cc i) -> p cc i", cc=8)
        sap = src[e].rearrange("(cc p) i -> p cc i", p=128)
        b.op("gpsimd", lambda E, d=dap, s=sap: E.dma_start(out=d, in_=s),
             [(sem, 16)])
        return (sem, b.mark(sem))

    if dbg and dbg >= 2:
        b.wait("gpsimd", "tap", tap2)
    b.wait("sync", "dv", wfm_done)
    b.wait("gpsimd", "dv", wfm_done)
    b.op("sync", lambda E: E.dma_start(out=s_eselb[:], in_=eselb),
         [("bc0", 16)])
    b.wait("tensor", "bc0", b.mark("bc0"))
    b.wait("vector", "dv", wfm_done)
    b.wait("tensor", "dv", wfm_done)
    gm, um, dmk = {}, {}, {}
    gm[0] = ldw(0, "g", wgbD[0])
    um[0] = ldw(0, "u", wubD[0])
    dmk[0] = ldw(0, "d", wdbD)
    acc_ready = {tt: 0 for tt in range(4)}
    silum = {}
    hmul = {}
    accev = {}
    downm = {}
    psg_last = {}
    psu_last = {}
    psd_last = {}
    pswD_last = [0]
    psw_last = {0: 0, 1: 0}
    prev_down = 0
    for e in range(E):
        if e == 1:
            b.phase("De1")
        if e == 3:
            b.phase("De3")
        if e + 1 < E:
            if e >= 1:
                b.wait("gpsimd", "pe", downm[(e - 1, 3)])
            gm[e + 1] = ldw(e + 1, "g", wgbD[(e + 1) % 2])
            um[e + 1] = ldw(e + 1, "u", wubD[(e + 1) % 2])
        b.wait("tensor", um[e][0], um[e][1])
        b.wait("tensor", gm[e][0], gm[e][1])
        for tt in range(4):
            if pswD_last[0]:
                b.wait("tensor", "dv", pswD_last[0])
            wf = s_wfm[0:8, tt * 512:(tt + 1) * 512]
            es = s_eselb[:, e * 128:(e + 1) * 128]
            b.op("tensor", lambda E, l=es, r=wf: E.matmul(
                pswD[:, :], l, r, start=True, stop=True), [("pe", 1)])
            bcm = b.mark("pe")
            b.wait("vector", "pe", bcm)
            if psw_last[tt % 2]:
                b.wait("vector", "dv", psw_last[tt % 2])
            b.op("vector", lambda E, o=pswS[tt % 2]: E.tensor_copy(
                o[:, :], pswD[:, :]), [("dv", 1)])
            pswD_last[0] = b.mark("dv")
            pswev = pswD_last[0]
            if e == 0 and tt == 0:
                b.phase("Dbc")
            if prev_down:
                b.wait("scalar", "pe", prev_down)
                b.wait("vector", "pe", prev_down)
            for ic in range(8):
                if psg_last.get(ic % 2):
                    b.wait("tensor", "ac", psg_last[ic % 2])
                for cc in range(8):
                    l = wgbD[e % 2][:, cc * 1024 + ic * 128:
                                    cc * 1024 + (ic + 1) * 128]
                    r = s_xlnfm[:, cc * NTOK + tt * 512:cc * NTOK + (tt + 1) * 512]
                    b.op("tensor", lambda E, l=l, r=r, cc=cc,
                         psg=psgD[ic % 2]: E.matmul(
                        psg[:, :], l, r, start=(cc == 0), stop=(cc == 7)),
                        [("pe", 1)] if cc == 7 else ())
                gmk = b.mark("pe")
                if psu_last.get(ic % 2):
                    b.wait("tensor", "dv", psu_last[ic % 2])
                for cc in range(8):
                    l = wubD[e % 2][:, cc * 1024 + ic * 128:
                                    cc * 1024 + (ic + 1) * 128]
                    r = s_xlnfm[:, cc * NTOK + tt * 512:cc * NTOK + (tt + 1) * 512]
                    b.op("tensor", lambda E, l=l, r=r, cc=cc,
                         psu=psuD[ic % 2]: E.matmul(
                        psu[:, :], l, r, start=(cc == 0), stop=(cc == 7)),
                        [("pe", 1)] if cc == 7 else ())
                umk = b.mark("pe")
                b.wait("scalar", "pe", gmk)
                hap = hsbD[ic]
                b.op("scalar", lambda E, h=hap, psg=psgD[ic % 2]: E.activation(
                    h[:, 0:512], psg[:, :], AF.Silu), [("ac", 1)])
                silum[(e, tt, ic)] = b.mark("ac")
                psg_last[ic % 2] = silum[(e, tt, ic)]
                b.wait("vector", "ac", silum[(e, tt, ic)])
                b.wait("vector", "pe", umk)
                b.op("vector", lambda E, h=hap, psu=psuD[ic % 2]:
                     E.tensor_tensor(
                    h[:, 0:512], h[:, 0:512], psu[:, :], ALU.mult),
                    [("dv", 1)])
                b.bar("vector")
                b.op("vector", lambda E, h=hap, o=pswS[tt % 2]: E.tensor_tensor(
                    h[:, 0:512], h[:, 0:512], o[:, :], ALU.mult), [("dv", 1)])
                hmul[(e, tt, ic)] = b.mark("dv")
                psw_last[tt % 2] = hmul[(e, tt, ic)]
                psu_last[ic % 2] = hmul[(e, tt, ic)]
                if e == 0 and tt == 0 and ic == 0:
                    b.phase("Dh0")
                if e == 0 and tt == 0 and ic == 7:
                    b.phase("Dh7")
            b.wait("tensor", dmk[e][0], dmk[e][1])
            b.wait("tensor", "dv", hmul[(e, tt, 7)])
            for dc in range(8):
                if (tt, dc) in accev and e > 0:
                    b.wait("tensor", "dv", accev[(tt, dc)])
                if psd_last.get(dc % 2):
                    b.wait("tensor", "dv", psd_last[dc % 2])
                for ic in range(8):
                    l = wdbD[:, ic * 1024 + dc * 128:ic * 1024 + (dc + 1) * 128]
                    r = hsbD[ic][:, 0:512]
                    b.op("tensor", lambda E, l=l, r=r, ic=ic,
                         psd=psdD[dc % 2]: E.matmul(
                        psd[:, :], l, r, start=(ic == 0),
                        stop=(ic == 7)), [("pe", 1)] if ic == 7 else ())
                dmm = b.mark("pe")
                if e == 0 and tt == 0 and dc == 0:
                    b.phase("Ddm")
                b.wait("vector", "pe", dmm)
                acc = accsD[tt][dc]
                if e == 0:
                    b.op("vector", lambda E, a=acc, psd=psdD[dc % 2]:
                         E.tensor_copy(a[:, :], psd[:, :]), [("dv", 1)])
                else:
                    b.op("vector", lambda E, a=acc, psd=psdD[dc % 2]:
                         E.tensor_tensor(
                        a[:, :], a[:, :], psd[:, :], ALU.add),
                        [("dv", 1)])
                accev[(tt, dc)] = b.mark("dv")
                psd_last[dc % 2] = accev[(tt, dc)]
                if e == 0 and tt == 0 and dc == 0:
                    b.phase("Dd0")
            downm[(e, tt)] = b.mark("pe")
            prev_down = downm[(e, tt)]
            if e == 0 and tt == 0:
                b.phase("Dtt0")
            acc_ready[tt] = b.mark("dv")
        if e + 1 < E:
            b.wait("gpsimd", "pe", downm[(e, 3)])
            dmk[e + 1] = ldw(e + 1, "d", wdbD)

    b.phase("Dfin")
    stq = {}
    trum = {}
    b.wait("sync", "x1s", x1_stored)
    for tt in range(4):
        b.wait("tensor", "dv", acc_ready[tt])
        for tb in range(4):
            ot = tt * 4 + tb
            dl = x1_dram[ot * 128:(ot + 1) * 128, :]
            b.op("sync", lambda E, d=x1tD, s=dl: E.dma_start(out=d[:], in_=s),
                 [("dfl", 16)])
            xm = b.mark("dfl")
            for cc in range(8):
                if trum:
                    b.wait("tensor", "dv", trum[max(trum)])
                o = pstrD[:, 0:128]
                i = accsD[tt][cc][:, tb * 128:(tb + 1) * 128]
                b.op("tensor", lambda E, o=o, i=i: E.transpose(o, i, s_identf[:]),
                     [("pe", 1)])
                trm = b.mark("pe")
                b.wait("vector", "pe", trm)
                if cc == 0:
                    b.wait("vector", "dfl", xm)
                    if ot >= 1:
                        b.wait("vector", "st", stq.get(ot - 1, 0))
                dap = outtD[:, cc * 128:(cc + 1) * 128]
                xap = x1tD[:, cc * 128:(cc + 1) * 128]
                b.op("vector", lambda E, d=dap, x=xap: E.tensor_tensor(
                    d, x, pstrD[:, 0:128], ALU.add),
                    [("dv", 1)])
                trum[(tt, tb, cc)] = b.mark("dv")
            om = b.mark("dv")
            b.wait("sync", "dv", om)
            dout = out_own[ot * 128:(ot + 1) * 128, :]
            b.op("sync", lambda E, d=dout: E.dma_start(out=d, in_=outtD[:]),
                 [("st", 16)])
            stq[ot] = b.mark("st")
    b.wait("sync", "st", b.mark("st"))
    ps5.close()
    d_stack.close()

    b.phase("D")
    opl = b.ops
    if truncate is not None:
        cut = dict(b.phase_marks)[truncate]
        opl = {e: b.ops[e][:cut[e]] for e in ENGS}
    block = ctx.enter_context(nc.Block())
    for eng in ENGS:
        def body(Eng, _ops=opl[eng]):
            for fn in _ops:
                fn(Eng)
        getattr(block, eng)(body)
    ctx.close()
    return nc


def _consts():
    identf = np.eye(128, dtype=np.float32)
    identb = np.eye(128, dtype=BF)
    onesb = np.ones((128, 128), dtype=BF)
    onesn = np.zeros((128, 65), dtype=BF)
    onesn[0:64, 0] = 1.0
    onesn[64:128, 64] = 1.0
    onesf = np.ones((1, 128), dtype=np.float32)
    trib = np.zeros((4, 128, 512), dtype=np.float32)
    p = np.arange(128)[:, None]
    fidx = np.arange(512)[None, :]
    for t in range(4):
        trib[t] = (128 * t + p <= fidx).astype(np.float32)
    trib = trib.astype(BF)
    eselb = np.zeros((8, E * 128), dtype=BF)
    for e in range(E):
        eselb[e, e * 128:(e + 1) * 128] = 1.0
    return identf, identb, onesb, onesn, onesf, trib, eselb


def _percore(m):
    biasmask = np.full((128, NBIAS), -30.0, dtype=np.float32)
    for j in range(4):
        qb = 512 * g_of(j, m)
        for i in range(SLOTQ[j]):
            if 128 * (i + 1) > qb:
                biasmask[:, OFFJ[j] + i] = -1e30
    kdidx = np.zeros((128, 4 * NP), dtype=np.uint32)
    vdidx = np.zeros((128, 16), dtype=np.uint32)
    for j in range(4):
        g = g_of(j, m)
        for pr in range(NP):
            rows = pr * 128 + np.arange(128)
            kdidx[:, j * NP + pr] = rows * 16 + g
        for t in range(4):
            vdidx[:, j * 4 + t] = 512 * g + 128 * t + np.arange(128)
    return biasmask, kdidx, vdidx


_NC_CACHE = {}


def kernel(L0_attn_x_q, L0_attn_w_q, L0_attn_w_k, L0_attn_w_v, L0_attn_w_o,
           L0_ffn_Wg, L0_ffn_Wu, L0_ffn_Wd, L0_ffn_Wr, L0_ffn_temperature):
    x = np.ascontiguousarray(np.asarray(L0_attn_x_q, dtype=np.float32))
    w_q = np.asarray(L0_attn_w_q, dtype=np.float32)
    w_k = np.asarray(L0_attn_w_k, dtype=np.float32)
    w_v = np.asarray(L0_attn_w_v, dtype=np.float32)
    w_o = np.ascontiguousarray(np.asarray(L0_attn_w_o, dtype=np.float32))
    Wg = np.ascontiguousarray(np.asarray(L0_ffn_Wg, dtype=np.float32))
    Wu = np.ascontiguousarray(np.asarray(L0_ffn_Wu, dtype=np.float32))
    Wd = np.ascontiguousarray(np.asarray(L0_ffn_Wd, dtype=np.float32))
    Wr = np.ascontiguousarray(np.asarray(L0_ffn_Wr, dtype=np.float32))
    tmp = np.asarray(L0_ffn_temperature, dtype=np.float32).reshape(1, 1)

    def pairT(w):
        return np.ascontiguousarray(
            w.reshape(NP, 2, DH, C).transpose(0, 3, 1, 2).reshape(NP, C, 128))

    wqT, wkT, wvT = pairT(w_q), pairT(w_k), pairT(w_v)
    wrp = np.ascontiguousarray(
        Wr.reshape(8, 128, E).transpose(1, 0, 2).reshape(128, 8 * E))
    identf, identb, onesb, onesn, onesf, trib, eselb = _consts()

    if "nc" not in _NC_CACHE:
        _NC_CACHE["nc"] = build_program()
    nc = _NC_CACHE["nc"]

    in_maps = []
    for r in range(8):
        s, m = divmod(r, 4)
        xs = x[s]
        xo = np.concatenate(
            [xs[512 * g_of(j, m):512 * (g_of(j, m) + 1)] for j in range(4)])
        biasmask, kdidx, vdidx = _percore(m)
        in_maps.append(dict(
            xfull=xs, xown=np.ascontiguousarray(xo), wqT=wqT, wkT=wkT, wvT=wvT,
            wo=w_o, wg=Wg, wu=Wu, wd=Wd, wr=wrp, temp=tmp,
            identf=identf, identb=identb, onesb=onesb, onesn=onesn, onesf=onesf,
            trib=trib, eselb=eselb, biasmask=biasmask, kdidx=kdidx, vdidx=vdidx))

    res = run_bass_kernel_spmd(nc, in_maps, core_ids=list(range(8)))
    out = np.empty((B, T, C), dtype=np.float32)
    for r in range(8):
        s, m = divmod(r, 4)
        oo = np.asarray(res.results[r]["out_own"])
        for j in range(4):
            g = g_of(j, m)
            out[s, 512 * g:512 * (g + 1)] = oo[512 * j:512 * (j + 1)]
    return out
